# version 07
# baseline (speedup 1.0000x reference)
import numpy as np
import ml_dtypes
import concourse.bass as bass
import concourse.mybir as mybir
from concourse.bass_utils import run_bass_kernel_spmd

F32 = mybir.dt.float32
BF16 = mybir.dt.bfloat16
AF = mybir.ActivationFunctionType
ALU = mybir.AluOpType
AX = mybir.AxisListType

SEQ, S, DM = 2, 2048, 2048
H, DH, C, NCH = 16, 128, 128, 16
IN, JT = 8192, 16384
T = SEQ * S
NO = DM // 128
LN_EPS = 1e-5
Z_EPS = 1e-6


class Sem:
    def __init__(self, nc, name):
        self.h = nc.alloc_semaphore(name)
        self.n = 0

    def bump(self, k=1):
        self.n += k
        return self.n


class Prog:
    def __init__(self):
        self.items = []

    def on(self, eng, fn):
        self.items.append((eng, fn))

    def realize(self, nc):
        with nc.Block() as block:
            for name in ("sync", "gpsimd", "scalar", "vector", "tensor"):
                def mk(name):
                    def run(e):
                        for (n, fn) in self.items:
                            if n == name:
                                fn(e)
                    return run
                getattr(block, name)(mk(name))


def build(upto=6, dbg=()):
    nc = bass.Bass("TRN2", target_bir_lowering=False)

    x_in = nc.dram_tensor("x", [SEQ, S, DM], F32, kind="ExternalInput").ap()
    wq_d = nc.dram_tensor("wq", [DM, DM], F32, kind="ExternalInput").ap()
    wk_d = nc.dram_tensor("wk", [DM, DM], F32, kind="ExternalInput").ap()
    wv_d = nc.dram_tensor("wv", [DM, DM], F32, kind="ExternalInput").ap()
    wo_d = nc.dram_tensor("wo", [DM, DM], F32, kind="ExternalInput").ap()
    bq_d = nc.dram_tensor("bq", [1, DM], F32, kind="ExternalInput").ap()
    bk_d = nc.dram_tensor("bk", [1, DM], F32, kind="ExternalInput").ap()
    bv_d = nc.dram_tensor("bv", [1, DM], F32, kind="ExternalInput").ap()
    bo_d = nc.dram_tensor("bo", [1, DM], F32, kind="ExternalInput").ap()
    wgu_d = nc.dram_tensor("wgu", [JT, DM], F32, kind="ExternalInput").ap()
    wd_d = nc.dram_tensor("wd", [DM, IN], F32, kind="ExternalInput").ap()
    mask_d = nc.dram_tensor("cmask", [128, 128], BF16, kind="ExternalInput").ap()

    out_d = nc.dram_tensor("out", [SEQ, S, DM], F32, kind="ExternalOutput").ap()

    x_bf = nc.dram_tensor("x_bf", [SEQ, S, DM], BF16, kind="Internal").ap()
    qf_s = nc.dram_tensor("qf_s", [SEQ, H, DH, S], BF16, kind="Internal").ap()
    kf_s = nc.dram_tensor("kf_s", [SEQ, H, S, DH], BF16, kind="Internal").ap()
    v_s = nc.dram_tensor("v_s", [SEQ, H, S, DH], BF16, kind="Internal").ap()
    y_s = nc.dram_tensor("y_s", [SEQ, S, DM], BF16, kind="Internal").ap()
    attn_s = nc.dram_tensor("attn_s", [DM, T], BF16, kind="Internal").ap()
    h1f_s = nc.dram_tensor("h1f_s", [T, DM], F32, kind="Internal").ap()
    h1b_s = nc.dram_tensor("h1b_s", [T, DM], BF16, kind="Internal").ap()
    wgu_bf = nc.dram_tensor("wgu_bf", [JT, DM], BF16, kind="Internal").ap()
    wd_bf = nc.dram_tensor("wd_bf", [DM, IN], BF16, kind="Internal").ap()
    wdt_s = nc.dram_tensor("wdt_s", [NO, IN // 128, 128, 128], BF16, kind="Internal").ap()
    gate_s = nc.dram_tensor("gate_s", [IN // 128, 128, T], BF16, kind="Internal").ap()
    a_s = nc.dram_tensor("a_s", [IN // 128, 128, T], BF16, kind="Internal").ap()
    ffn_s = nc.dram_tensor("ffn_s", [DM, T], BF16, kind="Internal").ap()

    dbg_outs = {}
    dbg_specs = {"qf": ([SEQ, H, DH, S], BF16, qf_s), "kf": ([SEQ, H, S, DH], BF16, kf_s),
                 "v": ([SEQ, H, S, DH], BF16, v_s), "y": ([SEQ, S, DM], BF16, y_s),
                 "attn": ([DM, T], BF16, attn_s), "h1": ([T, DM], F32, h1f_s),
                 "a": ([IN // 128, 128, T], BF16, a_s)}
    for name in dbg:
        shp, dt, _src = dbg_specs[name]
        dbg_outs[name] = nc.dram_tensor("dbg_" + name, shp, dt, kind="ExternalOutput").ap()

    A = (nc.sbuf_base + 511) & ~511
    LIMIT = nc.sbuf_top

    def at(name, off, shape, dtype):
        assert (A + off) % 32 == 0, (name, off)
        return nc.alloc_sbuf_tensor_at(name, shape, dtype, offset=A + off)

    def fit(off_end):
        assert A + off_end <= LIMIT, f"SBUF overflow: A={A} +{off_end} > {LIMIT}"

    ones512 = at("ones512", 0, [1, 512], BF16)
    onescol = at("onescol", 1024, [128, 1], BF16)
    maskt = at("maskt", 1056, [128, 128], BF16)
    bq_sb = at("bq_sb", 1312, [1, DM], BF16)
    bk_sb = at("bk_sb", 5408, [1, DM], BF16)
    bv_sb = at("bv_sb", 9504, [1, DM], BF16)
    bo_sb = at("bo_sb", 13600, [1, DM], BF16)
    lnst = at("lnst", 17696, [128, 16], F32)
    epsrow = at("epsrow", 17760, [1, 128], BF16)
    B = 18048

    wk_sb = at("wk_sb", B + 0, [128, 16 * 2048], BF16)
    wv_sb = at("wv_sb", B + 65536, [128, 16 * 2048], BF16)
    xT = at("xT", B + 131072, [128, 16 * 1024], BF16)
    wq_c = [at(f"wq_c{p}", B + 163840 + p * 4096, [128, 16 * 128], BF16) for p in range(2)]
    qstg = at("qstg", B + 172032, [128, 1024], BF16)
    kstg = at("kstg", B + 174080, [128, 2048], BF16)
    vstg = at("vstg", B + 178176, [128, 2048], BF16)
    tmin = at("tmin", B + 182272, [128, 512], F32)
    texp = at("texp", B + 184320, [128, 512], F32)
    fit(B + 186368)

    qf_h = [at(f"qf_h{p}", B + p * 4096, [128, 2048], BF16) for p in range(2)]
    kftm_h = [at(f"kftm_h{p}", B + 8192 + p * 4096, [128, 2048], BF16) for p in range(2)]
    kffm_h = [at(f"kffm_h{p}", B + 16384 + p * 4096, [128, 2048], BF16) for p in range(2)]
    v_h = [at(f"v_h{p}", B + 24576 + p * 4160, [128, 16 * 129], BF16) for p in range(2)]
    kvbf = [at(f"kvbf{p}", B + 32896 + p * 512, [128, 129], BF16) for p in range(2)]
    amask = [at(f"amask{p}", B + 33920 + p * 256, [128, 128], BF16) for p in range(2)]
    zinv = [at(f"zinv{p}", B + 34432 + p * 32, [128, 1], F32) for p in range(2)]
    ystg = [at(f"ystg{p}", B + 34496 + p * 512, [128, 256], BF16) for p in range(2)]

    yT = at("yT", B + 40960, [128, 16 * 2048], BF16)
    wo_c = [at(f"wo_c{p}", B + 106496 + p * 4096, [128, 16 * 128], BF16) for p in range(2)]
    attnstg = at("attnstg", B + 114688, [128, 2048], BF16)
    wdtr = [at(f"wdtr{p}", B + 120320 + p * 1024, [128, 512], BF16) for p in range(2)]

    TOP = B + 131072
    r1 = at("r1", TOP, [128, 2048], F32)
    sqt = at("sqt", TOP + 8192, [128, 2048], F32)
    attn_tm = at("attn_tm", TOP + 16384, [128, 2048], BF16)
    ffn_tm = at("ffn_tm", TOP + 16384, [128, 2048], BF16)
    x_tm = at("x_tm", TOP + 20480, [128, 2048], F32)
    h1f_rd = at("h1f_rd", TOP + 20480, [128, 2048], F32)
    h1f = at("h1f", TOP + 28672, [128, 2048], F32)
    outstg = at("outstg", TOP + 28672, [128, 2048], F32)
    h1b = at("h1b", TOP + 36864, [128, 2048], BF16)
    dstg = [at(f"dstg{p}", TOP + 36864 + p * 1024, [128, 512], BF16) for p in range(2)]
    wdt6 = [at(f"wdt6{p}", TOP + 40960 + p * 8192, [128, 32 * 128], BF16) for p in range(2)]
    fit(TOP + 57344)

    h1T = at("h1T", B + 0, [128, 16 * 2048], BF16)
    wt5 = [at(f"wt5{p}", B + 65536 + p * 16384, [128, 16 * 512], BF16) for p in range(2)]
    gstg = at("gstg", B + 98304, [128, 2048], BF16)
    ustg = at("ustg", B + 102400, [128, 2048], BF16)
    grd = at("grd", B + 106496, [128, 2048], BF16)

    apan = [at(f"apan{p}", B + p * 65536, [128, 64 * 512], BF16) for p in range(2)]

    psb = [nc.alloc_psum_tensor(f"psb{b}", [128, 512], F32) for b in range(8)]

    P = Prog()
    s_gc = Sem(nc, "s_gc")
    s_tr = Sem(nc, "s_tr")
    s_sc = Sem(nc, "s_sc")
    s_mm = Sem(nc, "s_mm")
    s_dve = Sem(nc, "s_dve")
    s_act = Sem(nc, "s_act")
    s_out = Sem(nc, "s_out")

    def op(eng, method, *args, w=(), inc=None, **kw):
        def f(e, method=method, args=args, kw=kw, w=tuple(w), inc=inc):
            for (h, v) in w:
                if v > 0:
                    e.wait_ge(h, v)
            ins = getattr(e, method)(*args, **kw)
            if inc is not None:
                ins.then_inc(inc[0], inc[1])
        P.on(eng, f)

    def t_mms(mms, w=(), inc=None):
        def f(e, mms=tuple(mms), w=tuple(w), inc=inc):
            for (h, v) in w:
                if v > 0:
                    e.wait_ge(h, v)
            ins = None
            for (out, lhsT, rhs, st, sp) in mms:
                ins = e.matmul(out, lhsT, rhs, start=st, stop=sp)
            if inc is not None:
                ins.then_inc(inc[0], inc[1])
        P.on("tensor", f)

    ps_free = [(s_dve.h, 0)] * 8

    def env():
        return dict(qf_s=qf_s, kf_s=kf_s, v_s=v_s, y_s=y_s, attn_s=attn_s,
                    h1f_s=h1f_s, a_s=a_s, s_sc=s_sc, s_tr=s_tr, s_gc=s_gc)

    def finish():
        if dbg:
            s_dbg = Sem(nc, "s_dbg")
            srcs = {"qf": qf_s, "kf": kf_s, "v": v_s, "y": y_s,
                    "attn": attn_s, "h1": h1f_s, "a": a_s}
            w = [(s.h, s.n) for s in (s_sc, s_tr, s_gc) if s.n]
            for nm in dbg:
                def f(e, out=dbg_outs[nm], in_=srcs[nm], w=tuple(w)):
                    for (h, v) in w:
                        e.wait_ge(h, v)
                    e.dma_start(out=out, in_=in_).then_inc(s_dbg.h, 16)
                P.on("gpsimd", f)
                s_dbg.bump(16)
            def fin(e, t=s_dbg.n):
                e.wait_ge(s_dbg.h, t)
            P.on("gpsimd", fin)
        P.realize(nc)
        return nc, dbg_outs

    op("vector", "memset", ones512[:], 1.0, inc=(s_dve.h, 1)); s_dve.bump()
    op("vector", "memset", onescol[:], 1.0, inc=(s_dve.h, 1))
    s_dve.bump()
    op("vector", "memset", epsrow[:], Z_EPS, inc=(s_dve.h, 1))
    t_consts = s_dve.bump()
    op("sync", "dma_start", out=maskt[:], in_=mask_d, inc=(s_tr.h, 16))
    t_mask = s_tr.bump(16)

    for (dst, src) in ((bq_sb, bq_d), (bk_sb, bk_d), (bv_sb, bv_d), (bo_sb, bo_d)):
        op("gpsimd", "dma_start", out=dst[:], in_=src, inc=(s_gc.h, 16))
        s_gc.bump(16)
    t_bias = s_gc.n

    t_xc = np.zeros((SEQ, 4), dtype=np.int64)
    for seq in range(SEQ):
        for ch in range(4):
            op("gpsimd", "dma_start", out=x_bf[seq, ch * 512:(ch + 1) * 512, :],
               in_=x_in[seq, ch * 512:(ch + 1) * 512, :], inc=(s_gc.h, 16))
            t_xc[seq, ch] = s_gc.bump(16)

    _wkv_jobs = [(dst, srcd, ic) for (dst, srcd) in ((wk_sb, wk_d), (wv_sb, wv_d))
                 for ic in range(16)]
    _wkv_i = {"i": 0}

    def emit_wkv(n):
        for _ in range(n):
            i = _wkv_i["i"]
            if i >= len(_wkv_jobs):
                return
            dst, srcd, ic = _wkv_jobs[i]
            op("gpsimd", "dma_start", out=dst[:, ic * 2048:(ic + 1) * 2048],
               in_=srcd[ic * 128:(ic + 1) * 128, :], inc=(s_gc.h, 16))
            s_gc.bump(16)
            _wkv_i["i"] += 1

    wgu_T = [0] * 32
    wd_T = [0] * 8
    _wc = {"g": 0, "d": 0}

    def emit_wcast(n):
        for _ in range(n):
            if _wc["g"] < 32:
                g = _wc["g"]
                op("gpsimd", "dma_start", out=wgu_bf[g * 512:(g + 1) * 512, :],
                   in_=wgu_d[g * 512:(g + 1) * 512, :], inc=(s_gc.h, 16))
                wgu_T[g] = s_gc.bump(16)
                _wc["g"] += 1
            elif _wc["d"] < 8:
                d = _wc["d"]
                op("gpsimd", "dma_start", out=wd_bf[d * 256:(d + 1) * 256, :],
                   in_=wd_d[d * 256:(d + 1) * 256, :], inc=(s_gc.h, 16))
                wd_T[d] = s_gc.bump(16)
                _wc["d"] += 1

    wq_free = [(s_mm.h, 0), (s_mm.h, 0)]
    qpan_T = np.zeros((SEQ, 2, NO), dtype=np.int64)
    kvpan_T = np.zeros((SEQ,), dtype=np.int64)
    t_grp_prev = 0
    qstg_free = (s_sc.h, 0)
    kstg_free = (s_sc.h, 0)
    vstg_free = (s_sc.h, 0)
    tmin_free = (s_act.h, 0)
    texp_free = (s_dve.h, 0)

    _wkv_done = [0]
    for seq in range(SEQ):
        for tg in range(2):
            xT_w = [(s_gc.h, int(t_xc[seq, 2 * tg + 1]))]
            if t_grp_prev:
                xT_w.append((s_mm.h, t_grp_prev))
            for i in range(16):
                op("sync", "dma_start", out=xT[:, i * 1024:(i + 1) * 1024],
                   in_=x_bf[seq, tg * 1024:(tg + 1) * 1024, i * 128:(i + 1) * 128],
                   transpose=True, w=(xT_w if i == 0 else ()), inc=(s_tr.h, 16))
                s_tr.bump(16)
            t_xT = s_tr.n
            t_xT2 = 0

            for o in range(NO):
                p = o % 2
                op("gpsimd", "dma_start", out=wq_c[p][:],
                   in_=wq_d[:, o * 128:(o + 1) * 128].rearrange("(ic p) o -> p ic o", p=128),
                   w=[wq_free[p]], inc=(s_gc.h, 16))
                t_wq = s_gc.bump(16)
                if seq == 0 and tg == 0:
                    emit_wkv(2)
                    if o == NO - 1:
                        _wkv_done[0] = s_gc.n

                banks = [2 * p, 2 * p + 1]
                mms = []
                for i in range(16):
                    for tt in range(2):
                        mms.append((psb[banks[tt]][:, :],
                                    wq_c[p][:, i * 128:(i + 1) * 128],
                                    xT[:, i * 1024 + tt * 512:i * 1024 + (tt + 1) * 512],
                                    i == 0, False))
                for tt in range(2):
                    mms.append((psb[banks[tt]][:, :],
                                bq_sb[0:1, o * 128:(o + 1) * 128],
                                ones512[0:1, 0:512], False, True))
                w = [(s_gc.h, t_wq), (s_tr.h, t_xT), (s_sc.h, t_xT2),
                     ps_free[banks[0]], ps_free[banks[1]]]
                if seq == 0 and tg == 0 and o == 0:
                    w += [(s_dve.h, t_consts), (s_gc.h, t_bias)]
                t_mms(mms, w=w, inc=(s_mm.h, 1))
                t_q = s_mm.bump()
                wq_free[p] = (s_mm.h, t_q)

                for tt in range(2):
                    b = banks[tt]
                    op("vector", "tensor_scalar_min", tmin[:], psb[b][:, :], 0.0,
                       w=[(s_mm.h, t_q), tmin_free], inc=(s_dve.h, 1))
                    t_min = s_dve.bump()
                    op("scalar", "activation", texp[:], tmin[:], AF.Exp,
                       w=[(s_dve.h, t_min), texp_free], inc=(s_act.h, 1))
                    t_exp = s_act.bump()
                    tmin_free = (s_act.h, t_exp)
                    op("vector", "scalar_tensor_tensor",
                       qstg[:, tt * 512:(tt + 1) * 512], psb[b][:, :], 0.0, texp[:],
                       op0=ALU.max, op1=ALU.add,
                       w=[(s_act.h, t_exp)] + ([qstg_free] if tt == 0 else []),
                       inc=(s_dve.h, 1))
                    t_stt = s_dve.bump()
                    texp_free = (s_dve.h, t_stt)
                    ps_free[b] = (s_dve.h, t_stt)
                op("scalar", "dma_start",
                   out=qf_s[seq, o, :, tg * 1024:(tg + 1) * 1024], in_=qstg[:],
                   w=[(s_dve.h, t_stt)], inc=(s_sc.h, 16))
                qpan_T[seq, tg, o] = s_sc.bump(16)
                qstg_free = (s_sc.h, int(qpan_T[seq, tg, o]))

            for cs in range(8):
                c0 = tg * 1024 + cs * 128
                xsl = lambda i: xT[:, i * 1024 + cs * 128:i * 1024 + cs * 128 + 128]
                mms = []
                for i in range(16):
                    for eg in range(4):
                        mms.append((psb[eg][:, :], xsl(i),
                                    wk_sb[:, i * 2048 + eg * 512:i * 2048 + (eg + 1) * 512],
                                    i == 0, False))
                    for eg in range(4):
                        mms.append((psb[4 + eg][:, :], xsl(i),
                                    wv_sb[:, i * 2048 + eg * 512:i * 2048 + (eg + 1) * 512],
                                    i == 0, False))
                for eg in range(4):
                    mms.append((psb[eg][:, :], ones512[0:1, 0:128],
                                bk_sb[0:1, eg * 512:(eg + 1) * 512], False, True))
                for eg in range(4):
                    mms.append((psb[4 + eg][:, :], ones512[0:1, 0:128],
                                bv_sb[0:1, eg * 512:(eg + 1) * 512], False, True))
                w = [ps_free[b] for b in range(8)] + [(s_gc.h, _wkv_done[0])]
                t_mms(mms, w=w, inc=(s_mm.h, 1))
                t_kv = s_mm.bump()

                for eg in range(4):
                    op("vector", "tensor_scalar_min", tmin[:], psb[eg][:, :], 0.0,
                       w=[(s_mm.h, t_kv), tmin_free], inc=(s_dve.h, 1))
                    t_min = s_dve.bump()
                    op("scalar", "activation", texp[:], tmin[:], AF.Exp,
                       w=[(s_dve.h, t_min), texp_free], inc=(s_act.h, 1))
                    t_exp = s_act.bump()
                    tmin_free = (s_act.h, t_exp)
                    op("vector", "scalar_tensor_tensor",
                       kstg[:, eg * 512:(eg + 1) * 512], psb[eg][:, :], 0.0, texp[:],
                       op0=ALU.max, op1=ALU.add,
                       w=[(s_act.h, t_exp)] + ([kstg_free] if eg == 0 else []),
                       inc=(s_dve.h, 1))
                    t_stt = s_dve.bump()
                    texp_free = (s_dve.h, t_stt)
                    ps_free[eg] = (s_dve.h, t_stt)
                    op("scalar", "activation", vstg[:, eg * 512:(eg + 1) * 512],
                       psb[4 + eg][:, :], AF.Copy,
                       w=[(s_mm.h, t_kv)] + ([vstg_free] if eg == 0 else []),
                       inc=(s_act.h, 1))
                    t_vc = s_act.bump()
                    ps_free[4 + eg] = (s_act.h, t_vc)
                op("scalar", "dma_start",
                   out=kf_s[seq, :, c0:c0 + 128, :].rearrange("h c e -> c h e"),
                   in_=kstg[:], w=[(s_dve.h, t_stt)], inc=(s_sc.h, 16))
                t_kw = s_sc.bump(16)
                kstg_free = (s_sc.h, t_kw)
                op("scalar", "dma_start",
                   out=v_s[seq, :, c0:c0 + 128, :].rearrange("h c e -> c h e"),
                   in_=vstg[:], w=[(s_act.h, t_vc)], inc=(s_sc.h, 16))
                t_vw = s_sc.bump(16)
                vstg_free = (s_sc.h, t_vw)
            t_grp_prev = s_mm.n
            emit_wcast(5)
        kvpan_T[seq] = s_sc.n

    if upto < 3:
        return finish()

    yst_T = np.zeros((SEQ,), dtype=np.int64)
    ystg_free = {0: (s_tr.h, 0), 1: (s_tr.h, 0)}
    p12_mm_end = s_mm.n
    tile_free = [(s_mm.h, p12_mm_end), (s_mm.h, p12_mm_end)]
    pair_mm_end = [p12_mm_end, p12_mm_end]
    trk = {}
    for seq in range(SEQ):
        pan_ready = int(max(kvpan_T[seq], qpan_T[seq].max()))

        if seq == 0:
            for par in range(2):
                op("vector", "memset",
                   v_h[par][:].rearrange("c (n e) -> c n e", e=129)[:, :, 128:129],
                   1.0, w=[(s_mm.h, p12_mm_end)], inc=(s_dve.h, 1))
            t_vones = s_dve.bump(2)

        for hp in range(8):
            mm_guard = s_mm.n
            dve_guard = (s_dve.h, s_dve.n)
            act_guard = (s_act.h, s_act.n)
            for hh in range(2):
                h = hp * 2 + hh
                par = hh
                op("scalar", "dma_start", out=qf_h[par][:], in_=qf_s[seq, h, :, :],
                   w=[tile_free[par], (s_sc.h, pan_ready)], inc=(s_sc.h, 16))
                s_sc.bump(16)
                op("scalar", "dma_start", out=kftm_h[par][:],
                   in_=kf_s[seq, h, :, :].rearrange("(n c) e -> c n e", c=128),
                   inc=(s_sc.h, 16))
                s_sc.bump(16)
                op("scalar", "dma_start",
                   out=v_h[par][:].rearrange("c (n e) -> c n e", e=129)[:, :, 0:128],
                   in_=v_s[seq, h, :, :].rearrange("(n c) e -> c n e", c=128),
                   inc=(s_sc.h, 16))
                ld_T = s_sc.bump(16)
                op("sync", "dma_start", out=kffm_h[par][:], in_=kf_s[seq, h, :, :],
                   transpose=True, w=[tile_free[par]], inc=(s_tr.h, 16))
                tr_T = s_tr.bump(16)

                op("vector", "memset", kvbf[par][:], 0.0,
                   w=[(s_mm.h, pair_mm_end[par])], inc=(s_dve.h, 1))
                t_state0 = s_dve.bump()

                trk[par] = dict(
                    kv_copy=(s_dve.h, t_state0),
                    mask=dve_guard,
                    amask_rd=(s_mm.h, mm_guard),
                    y_free={0: dve_guard, 1: dve_guard},
                    kvupd=(s_mm.h, mm_guard),
                    y_done=(s_mm.h, mm_guard),
                    ld=ld_T, tr=tr_T,
                )

            for ch in range(NCH):
                csl = slice(ch * 128, (ch + 1) * 128)
                vsl = slice(ch * 129, (ch + 1) * 129)
                tA, tY = {}, {}
                if ch > 0:
                    for hh in range(2):
                        tk = trk[hh]
                        op("scalar", "activation", kvbf[hh][:], psb[0 + hh][:, 0:129],
                           AF.Copy, w=[tk["kvupd"]], inc=(s_act.h, 1))
                        tk["kv_copy"] = (s_act.h, s_act.bump())
                for hh in range(2):
                    tk = trk[hh]
                    wA = [tk["mask"]]
                    if ch == 0:
                        wA += [(s_sc.h, tk["ld"]), (s_tr.h, tk["tr"]),
                               (s_tr.h, t_mask), act_guard, (s_dve.h, t_vones)]
                    t_mms([(psb[2 + hh][:, 0:128], kffm_h[hh][:, csl],
                            qf_h[hh][:, csl], True, True)], w=wA, inc=(s_mm.h, 1))
                    tA[hh] = s_mm.bump()
                for hh in range(2):
                    tk = trk[hh]
                    op("vector", "tensor_tensor", amask[hh][:], psb[2 + hh][:, 0:128],
                       maskt[:], op=ALU.mult,
                       w=[(s_mm.h, tA[hh]), tk["amask_rd"]], inc=(s_dve.h, 1))
                    tk["mask"] = (s_dve.h, s_dve.bump())
                for hh in range(2):
                    tk = trk[hh]
                    bY = 4 + 2 * hh + (ch % 2)
                    wy = [tk["mask"], tk["kv_copy"], tk["y_free"][ch % 2]]
                    if ch == 0:
                        wy.append(act_guard)
                    t_mms([
                        (psb[bY][:, 0:129], qf_h[hh][:, csl], kvbf[hh][:, 0:129],
                         True, False),
                        (psb[bY][:, 0:129], amask[hh][:], v_h[hh][:, vsl], False, False),
                        (psb[bY][:, 128:129], epsrow[0:1, :], ones512[0:1, 0:1],
                         False, True),
                    ], w=wy, inc=(s_mm.h, 1))
                    tY[hh] = s_mm.bump()
                    tk["amask_rd"] = (s_mm.h, tY[hh])
                    tk["y_done"] = (s_mm.h, tY[hh])
                for hh in range(2):
                    tk = trk[hh]
                    bY = 4 + 2 * hh + (ch % 2)
                    op("vector", "reciprocal", zinv[hh][:], psb[bY][:, 128:129],
                       w=[(s_mm.h, tY[hh])], inc=(s_dve.h, 1))
                    t_zi = s_dve.bump()
                    op("vector", "tensor_scalar_mul",
                       ystg[ch % 2][:, hh * 128:(hh + 1) * 128], psb[bY][:, 0:128],
                       zinv[hh][:, 0:1],
                       w=[(s_dve.h, t_zi), ystg_free[ch % 2]],
                       inc=(s_dve.h, 1))
                    tk["y_free"][ch % 2] = (s_dve.h, s_dve.bump())
                if ch < NCH - 1:
                    for hh in range(2):
                        tk = trk[hh]
                        wup = [tk["kv_copy"]]
                        if ch == 0:
                            wup += [act_guard, dve_guard]
                        t_mms([(psb[0 + hh][:, 0:129], kftm_h[hh][:, csl],
                                v_h[hh][:, vsl], ch == 0, ch == NCH - 2)],
                              w=wup, inc=(s_mm.h, 1))
                        tk["kvupd"] = (s_mm.h, s_mm.bump())

                op("sync", "dma_start",
                   out=y_s[seq, ch * 128:(ch + 1) * 128, hp * 256:(hp + 1) * 256],
                   in_=ystg[ch % 2][:],
                   w=[trk[0]["y_free"][ch % 2], trk[1]["y_free"][ch % 2]],
                   inc=(s_tr.h, 16))
                ystg_free[ch % 2] = (s_tr.h, s_tr.bump(16))

            for hh in range(2):
                tile_free[hh] = (s_mm.h, s_mm.n)
                pair_mm_end[hh] = s_mm.n
        yst_T[seq] = s_tr.n
        emit_wcast(6)

    p3_dve = (s_dve.h, s_dve.n)
    p3_act = (s_act.h, s_act.n)
    for b in range(8):
        ps_free[b] = p3_act if b < 4 else p3_dve
    ps_free[4] = ps_free[5] = p3_dve
    ps_free[6] = ps_free[7] = p3_dve
    ps_free[0] = ps_free[1] = p3_act
    ps_free[2] = ps_free[3] = p3_act

    if upto < 4:
        return finish()

    wo_free = [(s_mm.h, p12_mm_end), (s_mm.h, p12_mm_end)]
    attnst_T = np.zeros((SEQ,), dtype=np.int64)
    h1b_T = np.zeros((SEQ,), dtype=np.int64)
    h1f_T = np.zeros((SEQ, 16), dtype=np.int64)
    yT_loaded_mm = 0
    attnstg_free = (s_sc.h, 0)
    ln_guard = [(s_dve.h, 0)]
    ln_store_guard = []

    def emit_ln(in_bf_tile, in_f32_tile, w_in, out_f32, out_b16, stores):
        nonlocal ln_guard, ln_store_guard
        op("vector", "tensor_add", r1[:], in_bf_tile[:], in_f32_tile[:],
           w=list(w_in) + ln_guard, inc=(s_dve.h, 1))
        t_r = s_dve.bump()
        op("vector", "reduce_sum", lnst[:, 0:1], r1[:], axis=AX.X,
           w=[(s_dve.h, t_r)], inc=(s_dve.h, 1))
        t_s0 = s_dve.bump()
        op("scalar", "activation", sqt[:], r1[:], AF.Square,
           w=[(s_dve.h, t_r)], inc=(s_act.h, 1))
        t_sq = s_act.bump()
        op("vector", "reduce_sum", lnst[:, 1:2], sqt[:], axis=AX.X,
           w=[(s_act.h, t_sq), (s_dve.h, t_s0)], inc=(s_dve.h, 1))
        t_s1 = s_dve.bump()
        op("vector", "tensor_scalar_mul", lnst[:, 2:3], lnst[:, 0:1], 1.0 / DM,
           w=[(s_dve.h, t_s1)], inc=(s_dve.h, 1))
        t_m = s_dve.bump()
        op("vector", "tensor_scalar_mul", lnst[:, 3:4], lnst[:, 1:2], 1.0 / DM,
           w=[(s_dve.h, t_m)], inc=(s_dve.h, 1))
        t_m2 = s_dve.bump()
        op("vector", "tensor_tensor", lnst[:, 4:5], lnst[:, 2:3], lnst[:, 2:3],
           op=ALU.mult, w=[(s_dve.h, t_m2)], inc=(s_dve.h, 1))
        t_mm2 = s_dve.bump()
        op("vector", "tensor_sub", lnst[:, 5:6], lnst[:, 3:4], lnst[:, 4:5],
           w=[(s_dve.h, t_mm2)], inc=(s_dve.h, 1))
        t_v = s_dve.bump()
        op("vector", "tensor_scalar_add", lnst[:, 5:6], lnst[:, 5:6], LN_EPS,
           w=[(s_dve.h, t_v)], inc=(s_dve.h, 1))
        t_ve = s_dve.bump()
        op("scalar", "activation", lnst[:, 6:7], lnst[:, 5:6], AF.Sqrt,
           w=[(s_dve.h, t_ve)], inc=(s_act.h, 1))
        t_sd = s_act.bump()
        op("vector", "reciprocal", lnst[:, 7:8], lnst[:, 6:7],
           w=[(s_act.h, t_sd)], inc=(s_dve.h, 1))
        t_rs = s_dve.bump()
        op("vector", "tensor_scalar_mul", lnst[:, 8:9], lnst[:, 2:3], -1.0,
           w=[(s_dve.h, t_rs)], inc=(s_dve.h, 1))
        t_nm = s_dve.bump()
        op("vector", "tensor_scalar", out_f32[:], r1[:], lnst[:, 8:9], lnst[:, 7:8],
           op0=ALU.add, op1=ALU.mult,
           w=[(s_dve.h, t_nm)] + ln_store_guard, inc=(s_dve.h, 1))
        t_norm = s_dve.bump()
        t_bf = None
        if out_b16 is not None:
            op("vector", "tensor_copy", out_b16[:], out_f32[:],
               w=[(s_dve.h, t_norm)], inc=(s_dve.h, 1))
            t_bf = s_dve.bump()
        ln_guard = [(s_dve.h, s_dve.n)]
        ln_store_guard = stores(t_norm, t_bf)

    for seq in range(SEQ):
        for E in range(16):
            w = [(s_tr.h, int(yst_T[seq]))]
            if E == 0 and seq > 0:
                w.append((s_mm.h, yT_loaded_mm))
            op("sync", "dma_start", out=yT[:, E * 2048:(E + 1) * 2048],
               in_=y_s[seq, :, E * 128:(E + 1) * 128], transpose=True,
               w=w, inc=(s_tr.h, 16))
            s_tr.bump(16)
        t_yT = s_tr.n
        t_yT2 = 0

        for o in range(NO):
            p = o % 2
            op("gpsimd", "dma_start", out=wo_c[p][:],
               in_=wo_d[:, o * 128:(o + 1) * 128].rearrange("(ic p) o -> p ic o", p=128),
               w=[wo_free[p]], inc=(s_gc.h, 16))
            t_wo = s_gc.bump(16)

            banks = [4 * p + tt for tt in range(4)]
            mms = []
            for E in range(16):
                for tt in range(4):
                    mms.append((psb[banks[tt]][:, :],
                                wo_c[p][:, E * 128:(E + 1) * 128],
                                yT[:, E * 2048 + tt * 512:E * 2048 + (tt + 1) * 512],
                                E == 0, False))
            for tt in range(4):
                mms.append((psb[banks[tt]][:, :], bo_sb[0:1, o * 128:(o + 1) * 128],
                            ones512[0:1, 0:512], False, True))
            w = [(s_gc.h, t_wo), (s_tr.h, t_yT), (s_sc.h, t_yT2)] + [ps_free[b] for b in banks]
            t_mms(mms, w=w, inc=(s_mm.h, 1))
            t_woc = s_mm.bump()
            wo_free[p] = (s_mm.h, t_woc)

            for tt in range(4):
                b = banks[tt]
                op("scalar", "activation", attnstg[:, tt * 512:(tt + 1) * 512],
                   psb[b][:, :], AF.Copy,
                   w=[(s_mm.h, t_woc)] + ([attnstg_free] if tt == 0 else []),
                   inc=(s_act.h, 1))
                t_ev = s_act.bump()
                ps_free[b] = (s_act.h, t_ev)
            op("scalar", "dma_start",
               out=attn_s[o * 128:(o + 1) * 128, seq * 2048:(seq + 1) * 2048],
               in_=attnstg[:], w=[(s_act.h, t_ev)], inc=(s_sc.h, 16))
            attnstg_free = (s_sc.h, s_sc.bump(16))
        attnst_T[seq] = s_sc.n
        yT_loaded_mm = s_mm.n

        attn_tm_free = [(s_dve.h, 0)]
        for tc in range(16):
            tglob = seq * 2048 + tc * 128
            op("sync", "dma_start", out=attn_tm[:],
               in_=attn_s[:, tglob:tglob + 128], transpose=True,
               w=[(s_sc.h, int(attnst_T[seq]))] + attn_tm_free, inc=(s_tr.h, 16))
            t_at = s_tr.bump(16)
            op("scalar", "dma_start", out=x_tm[:],
               in_=x_in[seq, tc * 128:(tc + 1) * 128, :],
               w=attn_tm_free, inc=(s_sc.h, 16))
            t_xt = s_sc.bump(16)

            def stores(t_norm, t_bf, seq=seq, tc=tc, tglob=tglob):
                op("scalar", "dma_start", out=h1f_s[tglob:tglob + 128, :], in_=h1f[:],
                   w=[(s_dve.h, t_norm)], inc=(s_sc.h, 16))
                h1f_T[seq, tc] = s_sc.bump(16)
                op("scalar", "dma_start", out=h1b_s[tglob:tglob + 128, :], in_=h1b[:],
                   w=[(s_dve.h, t_bf)], inc=(s_sc.h, 16))
                return [(s_sc.h, s_sc.bump(16))]

            emit_ln(attn_tm, x_tm, [(s_tr.h, t_at), (s_sc.h, t_xt)], h1f, h1b, stores)
            attn_tm_free = [(s_dve.h, s_dve.n)]
        h1b_T[seq] = s_sc.n
        emit_wcast(6)

    t_p4_mm_end = s_mm.n

    if upto < 5:
        return finish()

    emit_wcast(48)
    gate_T = np.zeros((64,), dtype=np.int64)
    a_T = np.zeros((64,), dtype=np.int64)
    wt_free = [(s_mm.h, t_p4_mm_end), (s_mm.h, t_p4_mm_end)]
    gstg_free = (s_sc.h, 0)
    ustg_free = (s_sc.h, 0)
    grd_free = (s_dve.h, 0)
    h1T_T = [0, 0]
    wdt_T = 0
    _wdt_jobs = [(og, jc) for og in range(4) for jc in range(64)]
    _wdt_i = {"i": 0}
    wdtr_free = [(s_sc.h, 0), (s_sc.h, 0)]

    def emit_wdt(n):
        nonlocal wdt_T
        for _ in range(n):
            i = _wdt_i["i"]
            if i >= len(_wdt_jobs):
                return
            og, jc = _wdt_jobs[i]
            p = i % 2
            op("sync", "dma_start", out=wdtr[p][:],
               in_=wd_bf[og * 512:(og + 1) * 512, jc * 128:(jc + 1) * 128],
               transpose=True,
               w=[(s_gc.h, wd_T[og * 2 + 1]), wdtr_free[p],
                  (s_mm.h, t_p4_mm_end if i < 2 else 0)],
               inc=(s_tr.h, 16))
            t_t = s_tr.bump(16)
            op("scalar", "dma_start",
               out=wdt_s[og * 4:(og + 1) * 4, jc, :, :].rearrange("oc j o -> j oc o"),
               in_=wdtr[p][:], w=[(s_tr.h, t_t)], inc=(s_sc.h, 16))
            wdt_T = s_sc.bump(16)
            wdtr_free[p] = (s_sc.h, wdt_T)
            _wdt_i["i"] += 1

    for th in range(2):
        for i in range(16):
            w = [(s_sc.h, int(h1b_T[th]))]
            if i == 0:
                w.append((s_mm.h, t_p4_mm_end if th == 0 else s_mm.n))
            op("sync", "dma_start", out=h1T[:, i * 2048:(i + 1) * 2048],
               in_=h1b_s[th * 2048:(th + 1) * 2048, i * 128:(i + 1) * 128],
               transpose=True, w=w, inc=(s_tr.h, 16))
            s_tr.bump(16)
        h1T_T[th] = s_tr.n
        h1T_T2 = 0

        for phase in range(2):
            for gg in range(16):
                g = phase * 16 + gg
                p = gg % 2
                for i in range(16):
                    w = [(s_gc.h, wgu_T[g])]
                    if i == 0:
                        w.append(wt_free[p])
                        if th == 0 and phase == 0 and gg == 0:
                            w.append((s_mm.h, t_p4_mm_end))
                    op("sync", "dma_start", out=wt5[p][:, i * 512:(i + 1) * 512],
                       in_=wgu_bf[g * 512:(g + 1) * 512, i * 128:(i + 1) * 128],
                       transpose=True, w=w, inc=(s_tr.h, 16))
                    s_tr.bump(16)
                t_wt = s_tr.n
                t_wt2 = 0

                for js in range(4):
                    jc = g * 4 + js
                    banks = [(js % 2) * 4 + tt for tt in range(4)]
                    mms = []
                    for i in range(16):
                        for tt in range(4):
                            mms.append((psb[banks[tt]][:, :],
                                        wt5[p][:, i * 512 + js * 128:i * 512 + (js + 1) * 128],
                                        h1T[:, i * 2048 + tt * 512:i * 2048 + (tt + 1) * 512],
                                        i == 0, i == 15))
                    w = [ps_free[b] for b in banks]
                    if js == 0:
                        w += [(s_tr.h, t_wt), (s_sc.h, t_wt2),
                              (s_tr.h, h1T_T[th]), (s_sc.h, h1T_T2)]
                    t_mms(mms, w=w, inc=(s_mm.h, 1))
                    t_g = s_mm.bump()
                    if js == 3:
                        wt_free[p] = (s_mm.h, t_g)

                    if phase == 0:
                        for tt in range(4):
                            b = banks[tt]
                            op("scalar", "activation", gstg[:, tt * 512:(tt + 1) * 512],
                               psb[b][:, :], AF.Silu,
                               w=[(s_mm.h, t_g)] + ([gstg_free] if tt == 0 else []),
                               inc=(s_act.h, 1))
                            t_ev = s_act.bump()
                            ps_free[b] = (s_act.h, t_ev)
                        op("scalar", "dma_start",
                           out=gate_s[jc, :, th * 2048:(th + 1) * 2048],
                           in_=gstg[:], w=[(s_act.h, t_ev)], inc=(s_sc.h, 16))
                        gate_T[jc] = s_sc.bump(16)
                        gstg_free = (s_sc.h, int(gate_T[jc]))
                    else:
                        jcu = jc - 64
                        op("scalar", "dma_start", out=grd[:],
                           in_=gate_s[jcu, :, th * 2048:(th + 1) * 2048],
                           w=[(s_sc.h, int(gate_T[jcu])), grd_free], inc=(s_sc.h, 16))
                        t_gr = s_sc.bump(16)
                        for tt in range(4):
                            b = banks[tt]
                            op("vector", "tensor_tensor",
                               ustg[:, tt * 512:(tt + 1) * 512], psb[b][:, :],
                               grd[:, tt * 512:(tt + 1) * 512], op=ALU.mult,
                               w=[(s_mm.h, t_g), (s_sc.h, t_gr)] +
                                 ([ustg_free] if tt == 0 else []),
                               inc=(s_dve.h, 1))
                            t_ev = s_dve.bump()
                            ps_free[b] = (s_dve.h, t_ev)
                        grd_free = (s_dve.h, t_ev)
                        op("scalar", "dma_start",
                           out=a_s[jcu, :, th * 2048:(th + 1) * 2048],
                           in_=ustg[:], w=[(s_dve.h, t_ev)], inc=(s_sc.h, 16))
                        a_T[jcu] = s_sc.bump(16)
                        ustg_free = (s_sc.h, int(a_T[jcu]))
                emit_wdt(4)

    t_p5_mm_end = s_mm.n
    emit_wdt(256)
    t_a_all = int(a_T.max())

    if upto < 6:
        return finish()

    wdt6_free = [(s_mm.h, 0), (s_mm.h, 0)]
    ffn_T = np.zeros((8,), dtype=np.int64)
    pair_mm6 = [0] * 4
    ffn_tm_free = [(s_dve.h, 0)]
    dstg_free = [(s_sc.h, 0), (s_sc.h, 0)]

    for tp in range(4):
        for half in range(2):
            tg = 2 * tp + half
            for q8 in range(8):
                w = []
                if q8 == 0:
                    w = [(s_sc.h, t_a_all), (s_sc.h, wdt_T)]
                    w.append((s_mm.h, t_p5_mm_end if tp == 0 else pair_mm6[tp - 1]))
                op("sync", "dma_start",
                   out=apan[half][:, q8 * 8 * 512:(q8 + 1) * 8 * 512],
                   in_=a_s[q8 * 8:(q8 + 1) * 8, :, tg * 512:(tg + 1) * 512]
                       .rearrange("jc j t -> j jc t"),
                   w=w, inc=(s_tr.h, 16))
                s_tr.bump(16)
        t_ap = s_tr.n

        for o in range(NO):
            bb = [(o % 2) * 2, (o % 2) * 2 + 1]
            t_d = None
            for hf in range(2):
                ph = (o * 2 + hf) % 2
                qn, sem = (("scalar", s_sc) if hf == 0 else ("sync", s_tr))
                op(qn, "dma_start", out=wdt6[ph][:],
                   in_=wdt_s[o, hf * 32:(hf + 1) * 32, :, :].rearrange("jc j o -> j jc o"),
                   w=[(s_sc.h, wdt_T), wdt6_free[ph]], inc=(sem.h, 16))
                t_wl = (sem.h, sem.bump(16))

                mms = []
                for jl in range(32):
                    jc = hf * 32 + jl
                    for half in range(2):
                        mms.append((psb[bb[half]][:, :],
                                    wdt6[ph][:, jl * 128:(jl + 1) * 128],
                                    apan[half][:, jc * 512:(jc + 1) * 512],
                                    jc == 0, jc == 63))
                w = [t_wl]
                if hf == 0:
                    w += [ps_free[bb[0]], ps_free[bb[1]]]
                    if o == 0:
                        w.append((s_tr.h, t_ap))
                t_mms(mms, w=w, inc=(s_mm.h, 1))
                t_d = s_mm.bump()
                wdt6_free[ph] = (s_mm.h, t_d)

            for half in range(2):
                tg = 2 * tp + half
                op("scalar", "activation", dstg[half][:], psb[bb[half]][:, :], AF.Copy,
                   w=[(s_mm.h, t_d), dstg_free[half]], inc=(s_act.h, 1))
                t_ev = s_act.bump()
                ps_free[bb[half]] = (s_act.h, t_ev)
                op("scalar", "dma_start",
                   out=ffn_s[o * 128:(o + 1) * 128, tg * 512:(tg + 1) * 512],
                   in_=dstg[half][:], w=[(s_act.h, t_ev)], inc=(s_sc.h, 16))
                dstg_free[half] = (s_sc.h, s_sc.bump(16))
        ffn_T[2 * tp] = ffn_T[2 * tp + 1] = s_sc.n
        pair_mm6[tp] = s_mm.n

        for q4 in range(8):
            tglob = tp * 1024 + q4 * 128
            seq, trow = tglob // 2048, tglob % 2048
            op("sync", "dma_start", out=ffn_tm[:],
               in_=ffn_s[:, tglob:tglob + 128], transpose=True,
               w=[(s_sc.h, int(ffn_T[tglob // 512]))] + ffn_tm_free, inc=(s_tr.h, 16))
            t_ft = s_tr.bump(16)
            op("scalar", "dma_start", out=h1f_rd[:],
               in_=h1f_s[tglob:tglob + 128, :],
               w=[(s_sc.h, int(h1f_T[seq, trow // 128]))] + ffn_tm_free,
               inc=(s_sc.h, 16))
            t_hr = s_sc.bump(16)

            def stores2(t_norm, t_bf, seq=seq, trow=trow):
                op("sync", "dma_start", out=out_d[seq, trow:trow + 128, :],
                   in_=outstg[:], w=[(s_dve.h, t_norm)], inc=(s_out.h, 16))
                return [(s_out.h, s_out.bump(16))]

            emit_ln(ffn_tm, h1f_rd, [(s_tr.h, t_ft), (s_sc.h, t_hr)],
                    outstg, None, stores2)
            ffn_tm_free = [(s_dve.h, s_dve.n)]

    op("sync", "wait_ge", s_out.h, s_out.n)
    return finish()


def kernel(L0_attn_x, L0_attn_Wq, L0_attn_bq, L0_attn_Wk, L0_attn_bk,
           L0_attn_Wv, L0_attn_bv, L0_attn_Wo, L0_attn_bo,
           L0_ffn_w_gate_up, L0_ffn_w_down, _upto=6, _dbg=(), _cores=8):
    x = np.ascontiguousarray(np.asarray(L0_attn_x, dtype=np.float32))
    wq = np.ascontiguousarray(np.asarray(L0_attn_Wq, dtype=np.float32))
    wk = np.ascontiguousarray(np.asarray(L0_attn_Wk, dtype=np.float32))
    wv = np.ascontiguousarray(np.asarray(L0_attn_Wv, dtype=np.float32))
    wo = np.ascontiguousarray(np.asarray(L0_attn_Wo, dtype=np.float32))
    bq = np.ascontiguousarray(np.asarray(L0_attn_bq, dtype=np.float32).reshape(1, DM))
    bk = np.ascontiguousarray(np.asarray(L0_attn_bk, dtype=np.float32).reshape(1, DM))
    bv = np.ascontiguousarray(np.asarray(L0_attn_bv, dtype=np.float32).reshape(1, DM))
    bo = np.ascontiguousarray(np.asarray(L0_attn_bo, dtype=np.float32).reshape(1, DM))
    wgu = np.ascontiguousarray(np.asarray(L0_ffn_w_gate_up, dtype=np.float32))
    wd = np.ascontiguousarray(np.asarray(L0_ffn_w_down, dtype=np.float32))
    cmask = np.triu(np.ones((128, 128))).astype(ml_dtypes.bfloat16)

    nc, dbg_outs = build(upto=_upto, dbg=_dbg)

    in_maps = []
    for c in range(_cores):
        in_maps.append(dict(
            x=x[2 * c:2 * c + 2], wq=wq, wk=wk, wv=wv, wo=wo,
            bq=bq, bk=bk, bv=bv, bo=bo, wgu=wgu, wd=wd, cmask=cmask,
        ))
    res = run_bass_kernel_spmd(nc, in_maps, core_ids=list(range(_cores)))
    if _dbg:
        return res
    return np.concatenate([r["out"] for r in res.results], axis=0)


# version 08
# speedup vs baseline: 1.1481x; 1.1481x over previous
import numpy as np
import ml_dtypes
import concourse.bass as bass
import concourse.mybir as mybir
from concourse.bass_utils import run_bass_kernel_spmd

F32 = mybir.dt.float32
BF16 = mybir.dt.bfloat16
AF = mybir.ActivationFunctionType
ALU = mybir.AluOpType
AX = mybir.AxisListType

SEQ, S, DM = 2, 2048, 2048
H, DH, C, NCH = 16, 128, 128, 16
IN, JT = 8192, 16384
T = SEQ * S
NO = DM // 128
LN_EPS = 1e-5
Z_EPS = 1e-6


class Sem:
    def __init__(self, nc, name):
        self.h = nc.alloc_semaphore(name)
        self.n = 0

    def bump(self, k=1):
        self.n += k
        return self.n


class Prog:
    def __init__(self):
        self.items = []

    def on(self, eng, fn):
        self.items.append((eng, fn))

    def realize(self, nc):
        with nc.Block() as block:
            for name in ("sync", "gpsimd", "scalar", "vector", "tensor"):
                def mk(name):
                    def run(e):
                        for (n, fn) in self.items:
                            if n == name:
                                fn(e)
                    return run
                getattr(block, name)(mk(name))


def build(upto=6, dbg=()):
    nc = bass.Bass("TRN2", target_bir_lowering=False)

    x_in = nc.dram_tensor("x", [SEQ, S, DM], F32, kind="ExternalInput").ap()
    wq_d = nc.dram_tensor("wq", [DM, DM], F32, kind="ExternalInput").ap()
    wk_d = nc.dram_tensor("wk", [DM, DM], F32, kind="ExternalInput").ap()
    wv_d = nc.dram_tensor("wv", [DM, DM], F32, kind="ExternalInput").ap()
    wo_d = nc.dram_tensor("wo", [DM, DM], F32, kind="ExternalInput").ap()
    bq_d = nc.dram_tensor("bq", [1, DM], F32, kind="ExternalInput").ap()
    bk_d = nc.dram_tensor("bk", [1, DM], F32, kind="ExternalInput").ap()
    bv_d = nc.dram_tensor("bv", [1, DM], F32, kind="ExternalInput").ap()
    bo_d = nc.dram_tensor("bo", [1, DM], F32, kind="ExternalInput").ap()
    wgu_d = nc.dram_tensor("wgu", [JT, DM], F32, kind="ExternalInput").ap()
    wd_d = nc.dram_tensor("wd", [DM, IN], F32, kind="ExternalInput").ap()
    mask_d = nc.dram_tensor("cmask", [128, 128], BF16, kind="ExternalInput").ap()

    out_d = nc.dram_tensor("out", [SEQ, S, DM], F32, kind="ExternalOutput").ap()

    x_bf = nc.dram_tensor("x_bf", [SEQ, S, DM], BF16, kind="Internal").ap()
    qf_s = nc.dram_tensor("qf_s", [SEQ, H, DH, S], BF16, kind="Internal").ap()
    kf_s = nc.dram_tensor("kf_s", [SEQ, H, S, DH], BF16, kind="Internal").ap()
    v_s = nc.dram_tensor("v_s", [SEQ, H, S, DH], BF16, kind="Internal").ap()
    y_s = nc.dram_tensor("y_s", [SEQ, S, DM], BF16, kind="Internal").ap()
    attn_s = nc.dram_tensor("attn_s", [DM, T], BF16, kind="Internal").ap()
    h1f_s = nc.dram_tensor("h1f_s", [T, DM], F32, kind="Internal").ap()
    h1b_s = nc.dram_tensor("h1b_s", [T, DM], BF16, kind="Internal").ap()
    wgu_bf = nc.dram_tensor("wgu_bf", [JT, DM], BF16, kind="Internal").ap()
    wd_bf = nc.dram_tensor("wd_bf", [DM, IN], BF16, kind="Internal").ap()
    wdt_s = nc.dram_tensor("wdt_s", [NO, IN // 128, 128, 128], BF16, kind="Internal").ap()
    gate_s = nc.dram_tensor("gate_s", [IN // 128, 128, T], BF16, kind="Internal").ap()
    a_s = nc.dram_tensor("a_s", [IN // 128, 128, T], BF16, kind="Internal").ap()
    ffn_s = nc.dram_tensor("ffn_s", [DM, T], BF16, kind="Internal").ap()

    dbg_outs = {}
    dbg_specs = {"qf": ([SEQ, H, DH, S], BF16, qf_s), "kf": ([SEQ, H, S, DH], BF16, kf_s),
                 "v": ([SEQ, H, S, DH], BF16, v_s), "y": ([SEQ, S, DM], BF16, y_s),
                 "attn": ([DM, T], BF16, attn_s), "h1": ([T, DM], F32, h1f_s),
                 "a": ([IN // 128, 128, T], BF16, a_s)}
    for name in dbg:
        shp, dt, _src = dbg_specs[name]
        dbg_outs[name] = nc.dram_tensor("dbg_" + name, shp, dt, kind="ExternalOutput").ap()

    A = (nc.sbuf_base + 511) & ~511
    LIMIT = nc.sbuf_top

    def at(name, off, shape, dtype):
        assert (A + off) % 32 == 0, (name, off)
        return nc.alloc_sbuf_tensor_at(name, shape, dtype, offset=A + off)

    def fit(off_end):
        assert A + off_end <= LIMIT, f"SBUF overflow: A={A} +{off_end} > {LIMIT}"

    ones512 = at("ones512", 0, [1, 512], BF16)
    onescol = at("onescol", 1024, [128, 1], BF16)
    maskt = at("maskt", 1056, [128, 128], BF16)
    bq_sb = at("bq_sb", 1312, [1, DM], BF16)
    bk_sb = at("bk_sb", 5408, [1, DM], BF16)
    bv_sb = at("bv_sb", 9504, [1, DM], BF16)
    bo_sb = at("bo_sb", 13600, [1, DM], BF16)
    lnst = at("lnst", 17696, [128, 16], F32)
    epsrow = at("epsrow", 17760, [1, 128], BF16)
    B = 18048

    wk_sb = at("wk_sb", B + 0, [128, 16 * 2048], BF16)
    wv_sb = at("wv_sb", B + 65536, [128, 16 * 2048], BF16)
    xT = at("xT", B + 131072, [128, 16 * 1024], BF16)
    wq_c = [at(f"wq_c{p}", B + 163840 + p * 4096, [128, 16 * 128], BF16) for p in range(2)]
    qstg = at("qstg", B + 172032, [128, 1024], BF16)
    kstg = at("kstg", B + 174080, [128, 2048], BF16)
    vstg = at("vstg", B + 178176, [128, 2048], BF16)
    tmin = at("tmin", B + 182272, [128, 512], F32)
    texp = at("texp", B + 184320, [128, 512], F32)
    fit(B + 186368)

    qf_h = [at(f"qf_h{p}", B + p * 4096, [128, 2048], BF16) for p in range(2)]
    kftm_h = [at(f"kftm_h{p}", B + 8192 + p * 4096, [128, 2048], BF16) for p in range(2)]
    kffm_h = [at(f"kffm_h{p}", B + 16384 + p * 4096, [128, 2048], BF16) for p in range(2)]
    v_h = [at(f"v_h{p}", B + 24576 + p * 4160, [128, 16 * 129], BF16) for p in range(2)]
    kvbf = [at(f"kvbf{p}", B + 32896 + p * 512, [128, 129], BF16) for p in range(2)]
    amask = [at(f"amask{p}", B + 33920 + p * 256, [128, 128], BF16) for p in range(2)]
    zinv = [at(f"zinv{p}", B + 34432 + p * 32, [128, 1], F32) for p in range(2)]
    ystg = [at(f"ystg{p}", B + 34496 + p * 512, [128, 256], BF16) for p in range(2)]

    yT = at("yT", B + 40960, [128, 16 * 2048], BF16)
    wo_c = [at(f"wo_c{p}", B + 106496 + p * 4096, [128, 16 * 128], BF16) for p in range(2)]
    attnstg = at("attnstg", B + 114688, [128, 2048], BF16)
    wdtr = [at(f"wdtr{p}", B + 120320 + p * 1024, [128, 512], BF16) for p in range(4)]

    TOP = B + 131072
    r1 = at("r1", TOP, [128, 2048], F32)
    sqt = at("sqt", TOP + 8192, [128, 2048], F32)
    attn_tm = at("attn_tm", TOP + 16384, [128, 2048], BF16)
    ffn_tm = at("ffn_tm", TOP + 16384, [128, 2048], BF16)
    x_tm = at("x_tm", TOP + 20480, [128, 2048], F32)
    h1f_rd = at("h1f_rd", TOP + 20480, [128, 2048], F32)
    h1f = at("h1f", TOP + 28672, [128, 2048], F32)
    outstg = at("outstg", TOP + 28672, [128, 2048], F32)
    h1b = at("h1b", TOP + 36864, [128, 2048], BF16)
    dstg = [at(f"dstg{p}", TOP + 36864 + p * 1024, [128, 512], BF16) for p in range(2)]
    wdt6 = [at(f"wdt6{p}", TOP + 40960 + p * 8192, [128, 32 * 128], BF16) for p in range(2)]
    fit(TOP + 57344)

    h1T = at("h1T", B + 0, [128, 16 * 2048], BF16)
    wt5 = [at(f"wt5{p}", B + 65536 + p * 16384, [128, 16 * 512], BF16) for p in range(2)]
    gstg = at("gstg", B + 98304, [128, 2048], BF16)
    ustg = at("ustg", B + 102400, [128, 2048], BF16)
    grd = at("grd", B + 106496, [128, 2048], BF16)

    apan = [at(f"apan{p}", B + p * 65536, [128, 64 * 512], BF16) for p in range(2)]

    psb = [nc.alloc_psum_tensor(f"psb{b}", [128, 512], F32) for b in range(8)]

    P = Prog()
    s_gc = Sem(nc, "s_gc")
    s_tr = Sem(nc, "s_tr")
    s_sc = Sem(nc, "s_sc")
    s_mm = Sem(nc, "s_mm")
    s_dve = Sem(nc, "s_dve")
    s_act = Sem(nc, "s_act")
    s_out = Sem(nc, "s_out")

    def op(eng, method, *args, w=(), inc=None, **kw):
        def f(e, method=method, args=args, kw=kw, w=tuple(w), inc=inc):
            for (h, v) in w:
                if v > 0:
                    e.wait_ge(h, v)
            ins = getattr(e, method)(*args, **kw)
            if inc is not None:
                ins.then_inc(inc[0], inc[1])
        P.on(eng, f)

    def t_mms(mms, w=(), inc=None):
        def f(e, mms=tuple(mms), w=tuple(w), inc=inc):
            for (h, v) in w:
                if v > 0:
                    e.wait_ge(h, v)
            ins = None
            for (out, lhsT, rhs, st, sp) in mms:
                ins = e.matmul(out, lhsT, rhs, start=st, stop=sp)
            if inc is not None:
                ins.then_inc(inc[0], inc[1])
        P.on("tensor", f)

    ps_free = [(s_dve.h, 0)] * 8

    def env():
        return dict(qf_s=qf_s, kf_s=kf_s, v_s=v_s, y_s=y_s, attn_s=attn_s,
                    h1f_s=h1f_s, a_s=a_s, s_sc=s_sc, s_tr=s_tr, s_gc=s_gc)

    def finish():
        if dbg:
            s_dbg = Sem(nc, "s_dbg")
            srcs = {"qf": qf_s, "kf": kf_s, "v": v_s, "y": y_s,
                    "attn": attn_s, "h1": h1f_s, "a": a_s}
            w = [(s.h, s.n) for s in (s_sc, s_tr, s_gc) if s.n]
            for nm in dbg:
                def f(e, out=dbg_outs[nm], in_=srcs[nm], w=tuple(w)):
                    for (h, v) in w:
                        e.wait_ge(h, v)
                    e.dma_start(out=out, in_=in_).then_inc(s_dbg.h, 16)
                P.on("gpsimd", f)
                s_dbg.bump(16)
            def fin(e, t=s_dbg.n):
                e.wait_ge(s_dbg.h, t)
            P.on("gpsimd", fin)
        P.realize(nc)
        return nc, dbg_outs

    op("vector", "memset", ones512[:], 1.0, inc=(s_dve.h, 1)); s_dve.bump()
    op("vector", "memset", onescol[:], 1.0, inc=(s_dve.h, 1))
    s_dve.bump()
    op("vector", "memset", epsrow[:], Z_EPS, inc=(s_dve.h, 1))
    t_consts = s_dve.bump()
    op("sync", "dma_start", out=maskt[:], in_=mask_d, inc=(s_tr.h, 16))
    t_mask = s_tr.bump(16)

    for (dst, src) in ((bq_sb, bq_d), (bk_sb, bk_d), (bv_sb, bv_d), (bo_sb, bo_d)):
        op("gpsimd", "dma_start", out=dst[:], in_=src, inc=(s_gc.h, 16))
        s_gc.bump(16)
    t_bias = s_gc.n

    t_xc = np.zeros((SEQ, 4), dtype=np.int64)
    _xc_jobs = [(seq, ch) for seq in range(SEQ) for ch in range(4)]
    _xc_i = {"i": 0}

    def emit_xc(n):
        for _ in range(n):
            i = _xc_i["i"]
            if i >= len(_xc_jobs):
                return
            seq, ch = _xc_jobs[i]
            op("gpsimd", "dma_start", out=x_bf[seq, ch * 512:(ch + 1) * 512, :],
               in_=x_in[seq, ch * 512:(ch + 1) * 512, :], inc=(s_gc.h, 16))
            t_xc[seq, ch] = s_gc.bump(16)
            _xc_i["i"] += 1

    emit_xc(2)

    _wkv_jobs = [(dst, srcd, ic) for (dst, srcd) in ((wk_sb, wk_d), (wv_sb, wv_d))
                 for ic in range(16)]
    _wkv_i = {"i": 0}

    def emit_wkv(n):
        for _ in range(n):
            i = _wkv_i["i"]
            if i >= len(_wkv_jobs):
                return
            dst, srcd, ic = _wkv_jobs[i]
            op("gpsimd", "dma_start", out=dst[:, ic * 2048:(ic + 1) * 2048],
               in_=srcd[ic * 128:(ic + 1) * 128, :], inc=(s_gc.h, 16))
            s_gc.bump(16)
            _wkv_i["i"] += 1

    wgu_T = [0] * 32
    wd_T = [0] * 8
    _wc = {"g": 0, "d": 0}

    def emit_wcast(n):
        for _ in range(n):
            if _wc["g"] < 32:
                g = _wc["g"]
                op("gpsimd", "dma_start", out=wgu_bf[g * 512:(g + 1) * 512, :],
                   in_=wgu_d[g * 512:(g + 1) * 512, :], inc=(s_gc.h, 16))
                wgu_T[g] = s_gc.bump(16)
                _wc["g"] += 1
            elif _wc["d"] < 8:
                d = _wc["d"]
                op("gpsimd", "dma_start", out=wd_bf[d * 256:(d + 1) * 256, :],
                   in_=wd_d[d * 256:(d + 1) * 256, :], inc=(s_gc.h, 16))
                wd_T[d] = s_gc.bump(16)
                _wc["d"] += 1

    wq_free = [(s_mm.h, 0), (s_mm.h, 0)]
    qpan_T = np.zeros((SEQ, 2, NO), dtype=np.int64)
    kvpan_T = np.zeros((SEQ,), dtype=np.int64)
    t_grp_prev = 0
    qstg_free = (s_sc.h, 0)
    kstg_free = (s_sc.h, 0)
    vstg_free = (s_sc.h, 0)
    tmin_free = (s_act.h, 0)
    texp_free = (s_dve.h, 0)

    _wkv_done = [0]
    for seq in range(SEQ):
        for tg in range(2):
            xT_w = [(s_gc.h, int(t_xc[seq, 2 * tg + 1]))]
            if t_grp_prev:
                xT_w.append((s_mm.h, t_grp_prev))
            for i in range(16):
                op("sync", "dma_start", out=xT[:, i * 1024:(i + 1) * 1024],
                   in_=x_bf[seq, tg * 1024:(tg + 1) * 1024, i * 128:(i + 1) * 128],
                   transpose=True, w=(xT_w if i == 0 else ()), inc=(s_tr.h, 16))
                s_tr.bump(16)
            t_xT = s_tr.n
            t_xT2 = 0

            for o in range(NO):
                p = o % 2
                op("gpsimd", "dma_start", out=wq_c[p][:],
                   in_=wq_d[:, o * 128:(o + 1) * 128].rearrange("(ic p) o -> p ic o", p=128),
                   w=[wq_free[p]], inc=(s_gc.h, 16))
                t_wq = s_gc.bump(16)
                if seq == 0 and tg == 0:
                    if o < 6:
                        emit_xc(1)
                    emit_wkv(2)
                    if o == NO - 1:
                        _wkv_done[0] = s_gc.n

                banks = [2 * p, 2 * p + 1]
                mms = []
                for i in range(16):
                    for tt in range(2):
                        mms.append((psb[banks[tt]][:, :],
                                    wq_c[p][:, i * 128:(i + 1) * 128],
                                    xT[:, i * 1024 + tt * 512:i * 1024 + (tt + 1) * 512],
                                    i == 0, False))
                for tt in range(2):
                    mms.append((psb[banks[tt]][:, :],
                                bq_sb[0:1, o * 128:(o + 1) * 128],
                                ones512[0:1, 0:512], False, True))
                w = [(s_gc.h, t_wq), (s_tr.h, t_xT), (s_sc.h, t_xT2),
                     ps_free[banks[0]], ps_free[banks[1]]]
                if seq == 0 and tg == 0 and o == 0:
                    w += [(s_dve.h, t_consts), (s_gc.h, t_bias)]
                t_mms(mms, w=w, inc=(s_mm.h, 1))
                t_q = s_mm.bump()
                wq_free[p] = (s_mm.h, t_q)

                for tt in range(2):
                    b = banks[tt]
                    op("vector", "tensor_scalar_min", tmin[:], psb[b][:, :], 0.0,
                       w=[(s_mm.h, t_q), tmin_free], inc=(s_dve.h, 1))
                    t_min = s_dve.bump()
                    op("scalar", "activation", texp[:], tmin[:], AF.Exp,
                       w=[(s_dve.h, t_min), texp_free], inc=(s_act.h, 1))
                    t_exp = s_act.bump()
                    tmin_free = (s_act.h, t_exp)
                    op("vector", "scalar_tensor_tensor",
                       qstg[:, tt * 512:(tt + 1) * 512], psb[b][:, :], 0.0, texp[:],
                       op0=ALU.max, op1=ALU.add,
                       w=[(s_act.h, t_exp)] + ([qstg_free] if tt == 0 else []),
                       inc=(s_dve.h, 1))
                    t_stt = s_dve.bump()
                    texp_free = (s_dve.h, t_stt)
                    ps_free[b] = (s_dve.h, t_stt)
                op("scalar", "dma_start",
                   out=qf_s[seq, o, :, tg * 1024:(tg + 1) * 1024], in_=qstg[:],
                   w=[(s_dve.h, t_stt)], inc=(s_sc.h, 16))
                qpan_T[seq, tg, o] = s_sc.bump(16)
                qstg_free = (s_sc.h, int(qpan_T[seq, tg, o]))

            for cs in range(8):
                c0 = tg * 1024 + cs * 128
                xsl = lambda i: xT[:, i * 1024 + cs * 128:i * 1024 + cs * 128 + 128]
                mms = []
                for i in range(16):
                    for eg in range(4):
                        mms.append((psb[eg][:, :], xsl(i),
                                    wk_sb[:, i * 2048 + eg * 512:i * 2048 + (eg + 1) * 512],
                                    i == 0, False))
                    for eg in range(4):
                        mms.append((psb[4 + eg][:, :], xsl(i),
                                    wv_sb[:, i * 2048 + eg * 512:i * 2048 + (eg + 1) * 512],
                                    i == 0, False))
                for eg in range(4):
                    mms.append((psb[eg][:, :], ones512[0:1, 0:128],
                                bk_sb[0:1, eg * 512:(eg + 1) * 512], False, True))
                for eg in range(4):
                    mms.append((psb[4 + eg][:, :], ones512[0:1, 0:128],
                                bv_sb[0:1, eg * 512:(eg + 1) * 512], False, True))
                w = [ps_free[b] for b in range(8)] + [(s_gc.h, _wkv_done[0])]
                t_mms(mms, w=w, inc=(s_mm.h, 1))
                t_kv = s_mm.bump()

                for eg in range(4):
                    op("vector", "tensor_scalar_min", tmin[:], psb[eg][:, :], 0.0,
                       w=[(s_mm.h, t_kv), tmin_free], inc=(s_dve.h, 1))
                    t_min = s_dve.bump()
                    op("scalar", "activation", texp[:], tmin[:], AF.Exp,
                       w=[(s_dve.h, t_min), texp_free], inc=(s_act.h, 1))
                    t_exp = s_act.bump()
                    tmin_free = (s_act.h, t_exp)
                    op("vector", "scalar_tensor_tensor",
                       kstg[:, eg * 512:(eg + 1) * 512], psb[eg][:, :], 0.0, texp[:],
                       op0=ALU.max, op1=ALU.add,
                       w=[(s_act.h, t_exp)] + ([kstg_free] if eg == 0 else []),
                       inc=(s_dve.h, 1))
                    t_stt = s_dve.bump()
                    texp_free = (s_dve.h, t_stt)
                    ps_free[eg] = (s_dve.h, t_stt)
                    op("scalar", "activation", vstg[:, eg * 512:(eg + 1) * 512],
                       psb[4 + eg][:, :], AF.Copy,
                       w=[(s_mm.h, t_kv)] + ([vstg_free] if eg == 0 else []),
                       inc=(s_act.h, 1))
                    t_vc = s_act.bump()
                    ps_free[4 + eg] = (s_act.h, t_vc)
                op("scalar", "dma_start",
                   out=kf_s[seq, :, c0:c0 + 128, :].rearrange("h c e -> c h e"),
                   in_=kstg[:], w=[(s_dve.h, t_stt)], inc=(s_sc.h, 16))
                t_kw = s_sc.bump(16)
                kstg_free = (s_sc.h, t_kw)
                op("scalar", "dma_start",
                   out=v_s[seq, :, c0:c0 + 128, :].rearrange("h c e -> c h e"),
                   in_=vstg[:], w=[(s_act.h, t_vc)], inc=(s_sc.h, 16))
                t_vw = s_sc.bump(16)
                vstg_free = (s_sc.h, t_vw)
            t_grp_prev = s_mm.n
            emit_wcast(2)
        kvpan_T[seq] = s_sc.n

    if upto < 3:
        return finish()

    yst_T = np.zeros((SEQ,), dtype=np.int64)
    ystg_free = {0: (s_tr.h, 0), 1: (s_tr.h, 0)}
    p12_mm_end = s_mm.n
    tile_free = [(s_mm.h, p12_mm_end), (s_mm.h, p12_mm_end)]
    pair_mm_end = [p12_mm_end, p12_mm_end]
    trk = {}
    for seq in range(SEQ):
        pan_ready = int(max(kvpan_T[seq], qpan_T[seq].max()))

        if seq == 0:
            for par in range(2):
                op("vector", "memset",
                   v_h[par][:].rearrange("c (n e) -> c n e", e=129)[:, :, 128:129],
                   1.0, w=[(s_mm.h, p12_mm_end)], inc=(s_dve.h, 1))
            t_vones = s_dve.bump(2)

        for hp in range(8):
            mm_guard = s_mm.n
            dve_guard = (s_dve.h, s_dve.n)
            act_guard = (s_act.h, s_act.n)
            for hh in range(2):
                h = hp * 2 + hh
                par = hh
                op("scalar", "dma_start", out=qf_h[par][:], in_=qf_s[seq, h, :, :],
                   w=[tile_free[par], (s_sc.h, pan_ready)], inc=(s_sc.h, 16))
                s_sc.bump(16)
                op("scalar", "dma_start", out=kftm_h[par][:],
                   in_=kf_s[seq, h, :, :].rearrange("(n c) e -> c n e", c=128),
                   inc=(s_sc.h, 16))
                s_sc.bump(16)
                op("scalar", "dma_start",
                   out=v_h[par][:].rearrange("c (n e) -> c n e", e=129)[:, :, 0:128],
                   in_=v_s[seq, h, :, :].rearrange("(n c) e -> c n e", c=128),
                   inc=(s_sc.h, 16))
                ld_T = s_sc.bump(16)
                op("sync", "dma_start", out=kffm_h[par][:], in_=kf_s[seq, h, :, :],
                   transpose=True, w=[tile_free[par]], inc=(s_tr.h, 16))
                tr_T = s_tr.bump(16)

                op("vector", "memset", kvbf[par][:], 0.0,
                   w=[(s_mm.h, pair_mm_end[par])], inc=(s_dve.h, 1))
                t_state0 = s_dve.bump()

                trk[par] = dict(
                    kv_copy=(s_dve.h, t_state0),
                    mask=dve_guard,
                    amask_rd=(s_mm.h, mm_guard),
                    y_free={0: dve_guard, 1: dve_guard},
                    kvupd=(s_mm.h, mm_guard),
                    y_done=(s_mm.h, mm_guard),
                    ld=ld_T, tr=tr_T,
                )

            for ch in range(NCH):
                csl = slice(ch * 128, (ch + 1) * 128)
                vsl = slice(ch * 129, (ch + 1) * 129)
                tA, tY = {}, {}
                if ch > 0:
                    for hh in range(2):
                        tk = trk[hh]
                        op("scalar", "activation", kvbf[hh][:], psb[0 + hh][:, 0:129],
                           AF.Copy, w=[tk["kvupd"]], inc=(s_act.h, 1))
                        tk["kv_copy"] = (s_act.h, s_act.bump())
                for hh in range(2):
                    tk = trk[hh]
                    wA = [tk["mask"]]
                    if ch == 0:
                        wA += [(s_sc.h, tk["ld"]), (s_tr.h, tk["tr"]),
                               (s_tr.h, t_mask), act_guard, (s_dve.h, t_vones)]
                    t_mms([(psb[2 + hh][:, 0:128], kffm_h[hh][:, csl],
                            qf_h[hh][:, csl], True, True)], w=wA, inc=(s_mm.h, 1))
                    tA[hh] = s_mm.bump()
                for hh in range(2):
                    tk = trk[hh]
                    op("vector", "tensor_tensor", amask[hh][:], psb[2 + hh][:, 0:128],
                       maskt[:], op=ALU.mult,
                       w=[(s_mm.h, tA[hh]), tk["amask_rd"]], inc=(s_dve.h, 1))
                    tk["mask"] = (s_dve.h, s_dve.bump())
                for hh in range(2):
                    tk = trk[hh]
                    bY = 4 + 2 * hh + (ch % 2)
                    wy = [tk["mask"], tk["kv_copy"], tk["y_free"][ch % 2]]
                    if ch == 0:
                        wy.append(act_guard)
                    t_mms([
                        (psb[bY][:, 0:129], qf_h[hh][:, csl], kvbf[hh][:, 0:129],
                         True, False),
                        (psb[bY][:, 0:129], amask[hh][:], v_h[hh][:, vsl], False, False),
                        (psb[bY][:, 128:129], epsrow[0:1, :], ones512[0:1, 0:1],
                         False, True),
                    ], w=wy, inc=(s_mm.h, 1))
                    tY[hh] = s_mm.bump()
                    tk["amask_rd"] = (s_mm.h, tY[hh])
                    tk["y_done"] = (s_mm.h, tY[hh])
                for hh in range(2):
                    tk = trk[hh]
                    bY = 4 + 2 * hh + (ch % 2)
                    op("vector", "reciprocal", zinv[hh][:], psb[bY][:, 128:129],
                       w=[(s_mm.h, tY[hh])], inc=(s_dve.h, 1))
                    t_zi = s_dve.bump()
                    op("vector", "tensor_scalar_mul",
                       ystg[ch % 2][:, hh * 128:(hh + 1) * 128], psb[bY][:, 0:128],
                       zinv[hh][:, 0:1],
                       w=[(s_dve.h, t_zi), ystg_free[ch % 2]],
                       inc=(s_dve.h, 1))
                    tk["y_free"][ch % 2] = (s_dve.h, s_dve.bump())
                if ch < NCH - 1:
                    for hh in range(2):
                        tk = trk[hh]
                        wup = [tk["kv_copy"]]
                        if ch == 0:
                            wup += [act_guard, dve_guard]
                        t_mms([(psb[0 + hh][:, 0:129], kftm_h[hh][:, csl],
                                v_h[hh][:, vsl], ch == 0, ch == NCH - 2)],
                              w=wup, inc=(s_mm.h, 1))
                        tk["kvupd"] = (s_mm.h, s_mm.bump())

                op("sync", "dma_start",
                   out=y_s[seq, ch * 128:(ch + 1) * 128, hp * 256:(hp + 1) * 256],
                   in_=ystg[ch % 2][:],
                   w=[trk[0]["y_free"][ch % 2], trk[1]["y_free"][ch % 2]],
                   inc=(s_tr.h, 16))
                ystg_free[ch % 2] = (s_tr.h, s_tr.bump(16))

            for hh in range(2):
                tile_free[hh] = (s_mm.h, s_mm.n)
                pair_mm_end[hh] = s_mm.n
        yst_T[seq] = s_tr.n
        emit_wcast(10)

    p3_dve = (s_dve.h, s_dve.n)
    p3_act = (s_act.h, s_act.n)
    for b in range(8):
        ps_free[b] = p3_act if b < 4 else p3_dve
    ps_free[4] = ps_free[5] = p3_dve
    ps_free[6] = ps_free[7] = p3_dve
    ps_free[0] = ps_free[1] = p3_act
    ps_free[2] = ps_free[3] = p3_act

    if upto < 4:
        return finish()

    wo_free = [(s_mm.h, p12_mm_end), (s_mm.h, p12_mm_end)]
    attnst_T = np.zeros((SEQ,), dtype=np.int64)
    h1b_T = np.zeros((SEQ,), dtype=np.int64)
    h1f_T = np.zeros((SEQ, 16), dtype=np.int64)
    yT_loaded_mm = 0
    attnstg_free = (s_sc.h, 0)
    ln_guard = [(s_dve.h, 0)]
    ln_store_guard = []

    def emit_ln(in_bf_tile, in_f32_tile, w_in, out_f32, out_b16, stores):
        nonlocal ln_guard, ln_store_guard
        op("vector", "tensor_add", r1[:], in_bf_tile[:], in_f32_tile[:],
           w=list(w_in) + ln_guard, inc=(s_dve.h, 1))
        t_r = s_dve.bump()
        op("vector", "reduce_sum", lnst[:, 0:1], r1[:], axis=AX.X,
           w=[(s_dve.h, t_r)], inc=(s_dve.h, 1))
        t_s0 = s_dve.bump()
        op("scalar", "activation", sqt[:], r1[:], AF.Square,
           w=[(s_dve.h, t_r)], inc=(s_act.h, 1))
        t_sq = s_act.bump()
        op("vector", "reduce_sum", lnst[:, 1:2], sqt[:], axis=AX.X,
           w=[(s_act.h, t_sq), (s_dve.h, t_s0)], inc=(s_dve.h, 1))
        t_s1 = s_dve.bump()
        op("vector", "tensor_scalar_mul", lnst[:, 2:3], lnst[:, 0:1], 1.0 / DM,
           w=[(s_dve.h, t_s1)], inc=(s_dve.h, 1))
        t_m = s_dve.bump()
        op("vector", "tensor_scalar_mul", lnst[:, 3:4], lnst[:, 1:2], 1.0 / DM,
           w=[(s_dve.h, t_m)], inc=(s_dve.h, 1))
        t_m2 = s_dve.bump()
        op("vector", "tensor_tensor", lnst[:, 4:5], lnst[:, 2:3], lnst[:, 2:3],
           op=ALU.mult, w=[(s_dve.h, t_m2)], inc=(s_dve.h, 1))
        t_mm2 = s_dve.bump()
        op("vector", "tensor_sub", lnst[:, 5:6], lnst[:, 3:4], lnst[:, 4:5],
           w=[(s_dve.h, t_mm2)], inc=(s_dve.h, 1))
        t_v = s_dve.bump()
        op("vector", "tensor_scalar_add", lnst[:, 5:6], lnst[:, 5:6], LN_EPS,
           w=[(s_dve.h, t_v)], inc=(s_dve.h, 1))
        t_ve = s_dve.bump()
        op("scalar", "activation", lnst[:, 6:7], lnst[:, 5:6], AF.Sqrt,
           w=[(s_dve.h, t_ve)], inc=(s_act.h, 1))
        t_sd = s_act.bump()
        op("vector", "reciprocal", lnst[:, 7:8], lnst[:, 6:7],
           w=[(s_act.h, t_sd)], inc=(s_dve.h, 1))
        t_rs = s_dve.bump()
        op("vector", "tensor_scalar_mul", lnst[:, 8:9], lnst[:, 2:3], -1.0,
           w=[(s_dve.h, t_rs)], inc=(s_dve.h, 1))
        t_nm = s_dve.bump()
        op("vector", "tensor_scalar", out_f32[:], r1[:], lnst[:, 8:9], lnst[:, 7:8],
           op0=ALU.add, op1=ALU.mult,
           w=[(s_dve.h, t_nm)] + ln_store_guard, inc=(s_dve.h, 1))
        t_norm = s_dve.bump()
        t_bf = None
        if out_b16 is not None:
            op("vector", "tensor_copy", out_b16[:], out_f32[:],
               w=[(s_dve.h, t_norm)], inc=(s_dve.h, 1))
            t_bf = s_dve.bump()
        ln_guard = [(s_dve.h, s_dve.n)]
        ln_store_guard = stores(t_norm, t_bf)

    for seq in range(SEQ):
        for E in range(16):
            w = [(s_tr.h, int(yst_T[seq]))]
            if E == 0 and seq > 0:
                w.append((s_mm.h, yT_loaded_mm))
            op("sync", "dma_start", out=yT[:, E * 2048:(E + 1) * 2048],
               in_=y_s[seq, :, E * 128:(E + 1) * 128], transpose=True,
               w=w, inc=(s_tr.h, 16))
            s_tr.bump(16)
        t_yT = s_tr.n
        t_yT2 = 0

        for o in range(NO):
            p = o % 2
            op("gpsimd", "dma_start", out=wo_c[p][:],
               in_=wo_d[:, o * 128:(o + 1) * 128].rearrange("(ic p) o -> p ic o", p=128),
               w=[wo_free[p]], inc=(s_gc.h, 16))
            t_wo = s_gc.bump(16)

            banks = [4 * p + tt for tt in range(4)]
            mms = []
            for E in range(16):
                for tt in range(4):
                    mms.append((psb[banks[tt]][:, :],
                                wo_c[p][:, E * 128:(E + 1) * 128],
                                yT[:, E * 2048 + tt * 512:E * 2048 + (tt + 1) * 512],
                                E == 0, False))
            for tt in range(4):
                mms.append((psb[banks[tt]][:, :], bo_sb[0:1, o * 128:(o + 1) * 128],
                            ones512[0:1, 0:512], False, True))
            w = [(s_gc.h, t_wo), (s_tr.h, t_yT), (s_sc.h, t_yT2)] + [ps_free[b] for b in banks]
            t_mms(mms, w=w, inc=(s_mm.h, 1))
            t_woc = s_mm.bump()
            wo_free[p] = (s_mm.h, t_woc)

            for tt in range(4):
                b = banks[tt]
                op("scalar", "activation", attnstg[:, tt * 512:(tt + 1) * 512],
                   psb[b][:, :], AF.Copy,
                   w=[(s_mm.h, t_woc)] + ([attnstg_free] if tt == 0 else []),
                   inc=(s_act.h, 1))
                t_ev = s_act.bump()
                ps_free[b] = (s_act.h, t_ev)
            op("scalar", "dma_start",
               out=attn_s[o * 128:(o + 1) * 128, seq * 2048:(seq + 1) * 2048],
               in_=attnstg[:], w=[(s_act.h, t_ev)], inc=(s_sc.h, 16))
            attnstg_free = (s_sc.h, s_sc.bump(16))
        attnst_T[seq] = s_sc.n
        yT_loaded_mm = s_mm.n

        attn_tm_free = [(s_dve.h, 0)]
        for tc in range(16):
            tglob = seq * 2048 + tc * 128
            op("sync", "dma_start", out=attn_tm[:],
               in_=attn_s[:, tglob:tglob + 128], transpose=True,
               w=[(s_sc.h, int(attnst_T[seq]))] + attn_tm_free, inc=(s_tr.h, 16))
            t_at = s_tr.bump(16)
            op("scalar", "dma_start", out=x_tm[:],
               in_=x_in[seq, tc * 128:(tc + 1) * 128, :],
               w=attn_tm_free, inc=(s_sc.h, 16))
            t_xt = s_sc.bump(16)

            def stores(t_norm, t_bf, seq=seq, tc=tc, tglob=tglob):
                op("scalar", "dma_start", out=h1f_s[tglob:tglob + 128, :], in_=h1f[:],
                   w=[(s_dve.h, t_norm)], inc=(s_sc.h, 16))
                h1f_T[seq, tc] = s_sc.bump(16)
                op("scalar", "dma_start", out=h1b_s[tglob:tglob + 128, :], in_=h1b[:],
                   w=[(s_dve.h, t_bf)], inc=(s_sc.h, 16))
                return [(s_sc.h, s_sc.bump(16))]

            emit_ln(attn_tm, x_tm, [(s_tr.h, t_at), (s_sc.h, t_xt)], h1f, h1b, stores)
            attn_tm_free = [(s_dve.h, s_dve.n)]
        h1b_T[seq] = s_sc.n
        emit_wcast(6)
        emit_wcast(0)

    t_p4_mm_end = s_mm.n

    if upto < 5:
        return finish()

    emit_wcast(48)
    gate_T = np.zeros((64,), dtype=np.int64)
    a_T = np.zeros((64,), dtype=np.int64)
    wt_free = [(s_mm.h, t_p4_mm_end), (s_mm.h, t_p4_mm_end)]
    gstg_free = (s_sc.h, 0)
    ustg_free = (s_sc.h, 0)
    grd_free = (s_dve.h, 0)
    h1T_T = [0, 0]
    wdt_T = 0
    _wdt_jobs = [(og, jc) for og in range(4) for jc in range(64)]
    _wdt_i = {"i": 0}
    wdtr_free = [(s_gc.h, 0)] * 4
    wdt_T2 = 0

    def emit_wdt(n):
        nonlocal wdt_T, wdt_T2
        for _ in range(n):
            i = _wdt_i["i"]
            if i >= len(_wdt_jobs):
                return
            og, jc = _wdt_jobs[i]
            p = i % 4
            op("sync", "dma_start", out=wdtr[p][:],
               in_=wd_bf[og * 512:(og + 1) * 512, jc * 128:(jc + 1) * 128],
               transpose=True,
               w=[(s_gc.h, wd_T[og * 2 + 1]), wdtr_free[p],
                  (s_mm.h, t_p4_mm_end if i < 4 else 0)],
               inc=(s_tr.h, 16))
            t_t = s_tr.bump(16)
            op("gpsimd", "dma_start",
               out=wdt_s[og * 4:(og + 1) * 4, jc, :, :].rearrange("oc j o -> j oc o"),
               in_=wdtr[p][:], w=[(s_tr.h, t_t)], inc=(s_gc.h, 16))
            wdt_T2 = s_gc.bump(16)
            wdtr_free[p] = (s_gc.h, wdt_T2)
            wdt_T = wdt_T2
            _wdt_i["i"] += 1

    for th in range(2):
        for i in range(16):
            w = [(s_sc.h, int(h1b_T[th]))]
            if i == 0:
                w.append((s_mm.h, t_p4_mm_end if th == 0 else s_mm.n))
            op("sync", "dma_start", out=h1T[:, i * 2048:(i + 1) * 2048],
               in_=h1b_s[th * 2048:(th + 1) * 2048, i * 128:(i + 1) * 128],
               transpose=True, w=w, inc=(s_tr.h, 16))
            s_tr.bump(16)
        h1T_T[th] = s_tr.n
        h1T_T2 = 0

        for phase in range(2):
            for gg in range(16):
                g = phase * 16 + gg
                p = gg % 2
                for i in range(16):
                    w = [(s_gc.h, wgu_T[g])]
                    if i == 0:
                        w.append(wt_free[p])
                        if th == 0 and phase == 0 and gg == 0:
                            w.append((s_mm.h, t_p4_mm_end))
                    op("sync", "dma_start", out=wt5[p][:, i * 512:(i + 1) * 512],
                       in_=wgu_bf[g * 512:(g + 1) * 512, i * 128:(i + 1) * 128],
                       transpose=True, w=w, inc=(s_tr.h, 16))
                    s_tr.bump(16)
                t_wt = s_tr.n
                t_wt2 = 0

                for js in range(4):
                    jc = g * 4 + js
                    banks = [(js % 2) * 4 + tt for tt in range(4)]
                    mms = []
                    for i in range(16):
                        for tt in range(4):
                            mms.append((psb[banks[tt]][:, :],
                                        wt5[p][:, i * 512 + js * 128:i * 512 + (js + 1) * 128],
                                        h1T[:, i * 2048 + tt * 512:i * 2048 + (tt + 1) * 512],
                                        i == 0, i == 15))
                    w = [ps_free[b] for b in banks]
                    if js == 0:
                        w += [(s_tr.h, t_wt), (s_sc.h, t_wt2),
                              (s_tr.h, h1T_T[th]), (s_sc.h, h1T_T2)]
                    t_mms(mms, w=w, inc=(s_mm.h, 1))
                    t_g = s_mm.bump()
                    if js == 3:
                        wt_free[p] = (s_mm.h, t_g)

                    if phase == 0:
                        for tt in range(4):
                            b = banks[tt]
                            op("scalar", "activation", gstg[:, tt * 512:(tt + 1) * 512],
                               psb[b][:, :], AF.Silu,
                               w=[(s_mm.h, t_g)] + ([gstg_free] if tt == 0 else []),
                               inc=(s_act.h, 1))
                            t_ev = s_act.bump()
                            ps_free[b] = (s_act.h, t_ev)
                        op("scalar", "dma_start",
                           out=gate_s[jc, :, th * 2048:(th + 1) * 2048],
                           in_=gstg[:], w=[(s_act.h, t_ev)], inc=(s_sc.h, 16))
                        gate_T[jc] = s_sc.bump(16)
                        gstg_free = (s_sc.h, int(gate_T[jc]))
                    else:
                        jcu = jc - 64
                        op("scalar", "dma_start", out=grd[:],
                           in_=gate_s[jcu, :, th * 2048:(th + 1) * 2048],
                           w=[(s_sc.h, int(gate_T[jcu])), grd_free], inc=(s_sc.h, 16))
                        t_gr = s_sc.bump(16)
                        for tt in range(4):
                            b = banks[tt]
                            op("vector", "tensor_tensor",
                               ustg[:, tt * 512:(tt + 1) * 512], psb[b][:, :],
                               grd[:, tt * 512:(tt + 1) * 512], op=ALU.mult,
                               w=[(s_mm.h, t_g), (s_sc.h, t_gr)] +
                                 ([ustg_free] if tt == 0 else []),
                               inc=(s_dve.h, 1))
                            t_ev = s_dve.bump()
                            ps_free[b] = (s_dve.h, t_ev)
                        grd_free = (s_dve.h, t_ev)
                        op("scalar", "dma_start",
                           out=a_s[jcu, :, th * 2048:(th + 1) * 2048],
                           in_=ustg[:], w=[(s_dve.h, t_ev)], inc=(s_sc.h, 16))
                        a_T[jcu] = s_sc.bump(16)
                        ustg_free = (s_sc.h, int(a_T[jcu]))
                emit_wdt(4)

    t_p5_mm_end = s_mm.n
    emit_wdt(256)
    t_a_all = int(a_T.max())

    if upto < 6:
        return finish()

    wdt6_free = [(s_mm.h, 0), (s_mm.h, 0)]
    ffn_T = np.zeros((8,), dtype=np.int64)
    pair_mm6 = [0] * 4
    ffn_tm_free = [(s_dve.h, 0)]
    dstg_free = [(s_sc.h, 0), (s_sc.h, 0)]

    ap_T = {}

    def emit_apan(tp):
        for part in range(2):
            for half in range(2):
                tg = 2 * tp + half
                for q4 in range(4):
                    q8 = part * 4 + q4
                    w = []
                    if part == 0 and half == 0 and q4 == 0:
                        w = [(s_sc.h, t_a_all), (s_gc.h, wdt_T),
                             (s_mm.h, t_p5_mm_end if tp == 0 else pair_mm6[tp - 1])]
                    op("sync", "dma_start",
                       out=apan[half][:, q8 * 8 * 512:(q8 + 1) * 8 * 512],
                       in_=a_s[q8 * 8:(q8 + 1) * 8, :, tg * 512:(tg + 1) * 512]
                           .rearrange("jc j t -> j jc t"),
                       w=w, inc=(s_tr.h, 16))
                    s_tr.bump(16)
            if part == 0:
                t_lo = s_tr.n
        ap_T[tp] = (t_lo, s_tr.n)

    emit_apan(0)
    for tp in range(4):
        t_ap_lo, t_ap_hi = ap_T[tp]

        for o in range(NO):
            bb = [(o % 2) * 2, (o % 2) * 2 + 1]
            t_d = None
            for hf in range(2):
                ph = (o * 2 + hf) % 2
                qn, sem = (("scalar", s_sc) if hf == 0 else ("sync", s_tr))
                op(qn, "dma_start", out=wdt6[ph][:],
                   in_=wdt_s[o, hf * 32:(hf + 1) * 32, :, :].rearrange("jc j o -> j jc o"),
                   w=[(s_gc.h, wdt_T), wdt6_free[ph]], inc=(sem.h, 16))
                t_wl = (sem.h, sem.bump(16))

                mms = []
                for jl in range(32):
                    jc = hf * 32 + jl
                    for half in range(2):
                        mms.append((psb[bb[half]][:, :],
                                    wdt6[ph][:, jl * 128:(jl + 1) * 128],
                                    apan[half][:, jc * 512:(jc + 1) * 512],
                                    jc == 0, jc == 63))
                w = [t_wl]
                if hf == 0:
                    w += [ps_free[bb[0]], ps_free[bb[1]]]
                    if o == 0:
                        w.append((s_tr.h, t_ap_lo))
                elif o == 0:
                    w.append((s_tr.h, t_ap_hi))
                t_mms(mms, w=w, inc=(s_mm.h, 1))
                t_d = s_mm.bump()
                wdt6_free[ph] = (s_mm.h, t_d)

            for half in range(2):
                tg = 2 * tp + half
                op("scalar", "activation", dstg[half][:], psb[bb[half]][:, :], AF.Copy,
                   w=[(s_mm.h, t_d), dstg_free[half]], inc=(s_act.h, 1))
                t_ev = s_act.bump()
                ps_free[bb[half]] = (s_act.h, t_ev)
                op("scalar", "dma_start",
                   out=ffn_s[o * 128:(o + 1) * 128, tg * 512:(tg + 1) * 512],
                   in_=dstg[half][:], w=[(s_act.h, t_ev)], inc=(s_sc.h, 16))
                dstg_free[half] = (s_sc.h, s_sc.bump(16))
        ffn_T[2 * tp] = ffn_T[2 * tp + 1] = s_sc.n
        pair_mm6[tp] = s_mm.n
        if tp < 3:
            emit_apan(tp + 1)

        for q4 in range(8):
            tglob = tp * 1024 + q4 * 128
            seq, trow = tglob // 2048, tglob % 2048
            op("sync", "dma_start", out=ffn_tm[:],
               in_=ffn_s[:, tglob:tglob + 128], transpose=True,
               w=[(s_sc.h, int(ffn_T[tglob // 512]))] + ffn_tm_free, inc=(s_tr.h, 16))
            t_ft = s_tr.bump(16)
            op("scalar", "dma_start", out=h1f_rd[:],
               in_=h1f_s[tglob:tglob + 128, :],
               w=[(s_sc.h, int(h1f_T[seq, trow // 128]))] + ffn_tm_free,
               inc=(s_sc.h, 16))
            t_hr = s_sc.bump(16)

            def stores2(t_norm, t_bf, seq=seq, trow=trow):
                op("sync", "dma_start", out=out_d[seq, trow:trow + 128, :],
                   in_=outstg[:], w=[(s_dve.h, t_norm)], inc=(s_out.h, 16))
                return [(s_out.h, s_out.bump(16))]

            emit_ln(ffn_tm, h1f_rd, [(s_tr.h, t_ft), (s_sc.h, t_hr)],
                    outstg, None, stores2)
            ffn_tm_free = [(s_dve.h, s_dve.n)]

    op("sync", "wait_ge", s_out.h, s_out.n)
    return finish()


def kernel(L0_attn_x, L0_attn_Wq, L0_attn_bq, L0_attn_Wk, L0_attn_bk,
           L0_attn_Wv, L0_attn_bv, L0_attn_Wo, L0_attn_bo,
           L0_ffn_w_gate_up, L0_ffn_w_down, _upto=6, _dbg=(), _cores=8):
    x = np.ascontiguousarray(np.asarray(L0_attn_x, dtype=np.float32))
    wq = np.ascontiguousarray(np.asarray(L0_attn_Wq, dtype=np.float32))
    wk = np.ascontiguousarray(np.asarray(L0_attn_Wk, dtype=np.float32))
    wv = np.ascontiguousarray(np.asarray(L0_attn_Wv, dtype=np.float32))
    wo = np.ascontiguousarray(np.asarray(L0_attn_Wo, dtype=np.float32))
    bq = np.ascontiguousarray(np.asarray(L0_attn_bq, dtype=np.float32).reshape(1, DM))
    bk = np.ascontiguousarray(np.asarray(L0_attn_bk, dtype=np.float32).reshape(1, DM))
    bv = np.ascontiguousarray(np.asarray(L0_attn_bv, dtype=np.float32).reshape(1, DM))
    bo = np.ascontiguousarray(np.asarray(L0_attn_bo, dtype=np.float32).reshape(1, DM))
    wgu = np.ascontiguousarray(np.asarray(L0_ffn_w_gate_up, dtype=np.float32))
    wd = np.ascontiguousarray(np.asarray(L0_ffn_w_down, dtype=np.float32))
    cmask = np.triu(np.ones((128, 128))).astype(ml_dtypes.bfloat16)

    nc, dbg_outs = build(upto=_upto, dbg=_dbg)

    in_maps = []
    for c in range(_cores):
        in_maps.append(dict(
            x=x[2 * c:2 * c + 2], wq=wq, wk=wk, wv=wv, wo=wo,
            bq=bq, bk=bk, bv=bv, bo=bo, wgu=wgu, wd=wd, cmask=cmask,
        ))
    res = run_bass_kernel_spmd(nc, in_maps, core_ids=list(range(_cores)))
    if _dbg:
        return res
    return np.concatenate([r["out"] for r in res.results], axis=0)


# version 09
# speedup vs baseline: 1.2282x; 1.0698x over previous
import numpy as np
import ml_dtypes
import concourse.bass as bass
import concourse.mybir as mybir
from concourse.bass_utils import run_bass_kernel_spmd

F32 = mybir.dt.float32
BF16 = mybir.dt.bfloat16
AF = mybir.ActivationFunctionType
ALU = mybir.AluOpType
AX = mybir.AxisListType

SEQ, S, DM = 2, 2048, 2048
H, DH, C, NCH = 16, 128, 128, 16
IN, JT = 8192, 16384
T = SEQ * S
NO = DM // 128
LN_EPS = 1e-5
Z_EPS = 1e-6


class Sem:
    def __init__(self, nc, name):
        self.h = nc.alloc_semaphore(name)
        self.n = 0

    def bump(self, k=1):
        self.n += k
        return self.n


class Prog:
    def __init__(self):
        self.items = []

    def on(self, eng, fn):
        self.items.append((eng, fn))

    def realize(self, nc):
        with nc.Block() as block:
            for name in ("sync", "gpsimd", "scalar", "vector", "tensor"):
                def mk(name):
                    def run(e):
                        for (n, fn) in self.items:
                            if n == name:
                                fn(e)
                    return run
                getattr(block, name)(mk(name))


def build(upto=6, dbg=()):
    nc = bass.Bass("TRN2", target_bir_lowering=False)

    x_in = nc.dram_tensor("x", [SEQ, S, DM], F32, kind="ExternalInput").ap()
    wq_d = nc.dram_tensor("wq", [DM, DM], F32, kind="ExternalInput").ap()
    wk_d = nc.dram_tensor("wk", [DM, DM], F32, kind="ExternalInput").ap()
    wv_d = nc.dram_tensor("wv", [DM, DM], F32, kind="ExternalInput").ap()
    wo_d = nc.dram_tensor("wo", [DM, DM], F32, kind="ExternalInput").ap()
    bq_d = nc.dram_tensor("bq", [1, DM], F32, kind="ExternalInput").ap()
    bk_d = nc.dram_tensor("bk", [1, DM], F32, kind="ExternalInput").ap()
    bv_d = nc.dram_tensor("bv", [1, DM], F32, kind="ExternalInput").ap()
    bo_d = nc.dram_tensor("bo", [1, DM], F32, kind="ExternalInput").ap()
    wgu_d = nc.dram_tensor("wgu", [JT, DM], F32, kind="ExternalInput").ap()
    wd_d = nc.dram_tensor("wd", [DM, IN], F32, kind="ExternalInput").ap()
    mask_d = nc.dram_tensor("cmask", [128, 128], BF16, kind="ExternalInput").ap()

    out_d = nc.dram_tensor("out", [SEQ, S, DM], F32, kind="ExternalOutput").ap()

    x_bf = nc.dram_tensor("x_bf", [SEQ, S, DM], BF16, kind="Internal").ap()
    qf_s = nc.dram_tensor("qf_s", [SEQ, H, DH, S], BF16, kind="Internal").ap()
    kf_s = nc.dram_tensor("kf_s", [SEQ, H, S, DH], BF16, kind="Internal").ap()
    v_s = nc.dram_tensor("v_s", [SEQ, H, S, DH], BF16, kind="Internal").ap()
    y_s = nc.dram_tensor("y_s", [SEQ, S, DM], BF16, kind="Internal").ap()
    attn_s = nc.dram_tensor("attn_s", [DM, T], BF16, kind="Internal").ap()
    h1f_s = nc.dram_tensor("h1f_s", [T, DM], F32, kind="Internal").ap()
    h1b_s = nc.dram_tensor("h1b_s", [T, DM], BF16, kind="Internal").ap()
    wgu_bf = nc.dram_tensor("wgu_bf", [JT, DM], BF16, kind="Internal").ap()
    wd_bf = nc.dram_tensor("wd_bf", [DM, IN], BF16, kind="Internal").ap()
    wdt_s = nc.dram_tensor("wdt_s", [NO, IN // 128, 128, 128], BF16, kind="Internal").ap()
    gate_s = nc.dram_tensor("gate_s", [IN // 128, 128, T], BF16, kind="Internal").ap()
    a_s = nc.dram_tensor("a_s", [IN // 128, 128, T], BF16, kind="Internal").ap()
    ffn_s = nc.dram_tensor("ffn_s", [DM, T], BF16, kind="Internal").ap()

    dbg_outs = {}
    dbg_specs = {"qf": ([SEQ, H, DH, S], BF16, qf_s), "kf": ([SEQ, H, S, DH], BF16, kf_s),
                 "v": ([SEQ, H, S, DH], BF16, v_s), "y": ([SEQ, S, DM], BF16, y_s),
                 "attn": ([DM, T], BF16, attn_s), "h1": ([T, DM], F32, h1f_s),
                 "a": ([IN // 128, 128, T], BF16, a_s)}
    for name in dbg:
        shp, dt, _src = dbg_specs[name]
        dbg_outs[name] = nc.dram_tensor("dbg_" + name, shp, dt, kind="ExternalOutput").ap()

    A = (nc.sbuf_base + 511) & ~511
    LIMIT = nc.sbuf_top

    def at(name, off, shape, dtype):
        assert (A + off) % 32 == 0, (name, off)
        return nc.alloc_sbuf_tensor_at(name, shape, dtype, offset=A + off)

    def fit(off_end):
        assert A + off_end <= LIMIT, f"SBUF overflow: A={A} +{off_end} > {LIMIT}"

    ones512 = at("ones512", 0, [1, 512], BF16)
    onescol = at("onescol", 1024, [128, 1], BF16)
    maskt = at("maskt", 1056, [128, 128], BF16)
    bq_sb = at("bq_sb", 1312, [1, DM], BF16)
    bk_sb = at("bk_sb", 5408, [1, DM], BF16)
    bv_sb = at("bv_sb", 9504, [1, DM], BF16)
    bo_sb = at("bo_sb", 13600, [1, DM], BF16)
    lnst = at("lnst", 17696, [128, 16], F32)
    epsrow = at("epsrow", 17760, [1, 128], BF16)
    B = 18048

    wk_sb = at("wk_sb", B + 0, [128, 16 * 2048], BF16)
    wv_sb = at("wv_sb", B + 65536, [128, 16 * 2048], BF16)
    xT = at("xT", B + 131072, [128, 16 * 1024], BF16)
    wq_c = [at(f"wq_c{p}", B + 163840 + p * 4096, [128, 16 * 128], BF16) for p in range(2)]
    qstg = at("qstg", B + 172032, [128, 1024], BF16)
    kstg = at("kstg", B + 174080, [128, 2048], BF16)
    vstg = at("vstg", B + 178176, [128, 2048], BF16)
    tmin = at("tmin", B + 182272, [128, 512], F32)
    texp = at("texp", B + 184320, [128, 512], F32)
    fit(B + 186368)

    qf_h = [at(f"qf_h{p}", B + p * 4096, [128, 2048], BF16) for p in range(2)]
    kftm_h = [at(f"kftm_h{p}", B + 8192 + p * 4096, [128, 2048], BF16) for p in range(2)]
    kffm_h = [at(f"kffm_h{p}", B + 16384 + p * 4096, [128, 2048], BF16) for p in range(2)]
    v_h = [at(f"v_h{p}", B + 24576 + p * 4160, [128, 16 * 129], BF16) for p in range(2)]
    kvbf = [at(f"kvbf{p}", B + 32896 + p * 512, [128, 129], BF16) for p in range(2)]
    amask = [at(f"amask{p}", B + 33920 + p * 256, [128, 128], BF16) for p in range(2)]
    zinv = [at(f"zinv{p}", B + 34432 + p * 32, [128, 1], F32) for p in range(2)]
    ystg = [at(f"ystg{p}", B + 34496 + p * 512, [128, 256], BF16) for p in range(2)]

    yT = at("yT", B + 40960, [128, 16 * 2048], BF16)
    wo_c = [at(f"wo_c{p}", B + 106496 + p * 4096, [128, 16 * 128], BF16) for p in range(2)]
    attnstg = at("attnstg", B + 114688, [128, 2048], BF16)
    wdtr = [at(f"wdtr{p}", B + 120320 + p * 1024, [128, 512], BF16) for p in range(4)]

    TOP = B + 131072
    r1 = at("r1", TOP, [128, 2048], F32)
    sqt = at("sqt", TOP + 8192, [128, 2048], F32)
    attn_tm = at("attn_tm", TOP + 16384, [128, 2048], BF16)
    ffn_tm = at("ffn_tm", TOP + 16384, [128, 2048], BF16)
    x_tm = at("x_tm", TOP + 20480, [128, 2048], F32)
    h1f_rd = at("h1f_rd", TOP + 20480, [128, 2048], F32)
    h1f = at("h1f", TOP + 28672, [128, 2048], F32)
    outstg = at("outstg", TOP + 28672, [128, 2048], F32)
    h1b = at("h1b", TOP + 36864, [128, 2048], BF16)
    dstg = [at(f"dstg{p}", TOP + 36864 + p * 1024, [128, 512], BF16) for p in range(2)]
    wdt6 = [at(f"wdt6{p}", TOP + 40960 + p * 8192, [128, 32 * 128], BF16) for p in range(2)]
    fit(TOP + 57344)

    h1T = at("h1T", B + 0, [128, 16 * 2048], BF16)
    wt5 = [at(f"wt5{p}", B + 65536 + p * 16384, [128, 16 * 512], BF16) for p in range(2)]
    gstg = at("gstg", B + 98304, [128, 2048], BF16)
    ustg = at("ustg", B + 102400, [128, 2048], BF16)
    grd = at("grd", B + 106496, [128, 2048], BF16)

    apan = [at(f"apan{p}", B + p * 65536, [128, 64 * 512], BF16) for p in range(2)]

    psb = [nc.alloc_psum_tensor(f"psb{b}", [128, 512], F32) for b in range(8)]

    P = Prog()
    s_gc = Sem(nc, "s_gc")
    s_tr = Sem(nc, "s_tr")
    s_sc = Sem(nc, "s_sc")
    s_mm = Sem(nc, "s_mm")
    s_dve = Sem(nc, "s_dve")
    s_act = Sem(nc, "s_act")
    s_out = Sem(nc, "s_out")

    def op(eng, method, *args, w=(), inc=None, **kw):
        def f(e, method=method, args=args, kw=kw, w=tuple(w), inc=inc):
            for (h, v) in w:
                if v > 0:
                    e.wait_ge(h, v)
            ins = getattr(e, method)(*args, **kw)
            if inc is not None:
                ins.then_inc(inc[0], inc[1])
        P.on(eng, f)

    def t_mms(mms, w=(), inc=None):
        def f(e, mms=tuple(mms), w=tuple(w), inc=inc):
            for (h, v) in w:
                if v > 0:
                    e.wait_ge(h, v)
            ins = None
            for (out, lhsT, rhs, st, sp) in mms:
                ins = e.matmul(out, lhsT, rhs, start=st, stop=sp)
            if inc is not None:
                ins.then_inc(inc[0], inc[1])
        P.on("tensor", f)

    ps_free = [(s_dve.h, 0)] * 8

    def env():
        return dict(qf_s=qf_s, kf_s=kf_s, v_s=v_s, y_s=y_s, attn_s=attn_s,
                    h1f_s=h1f_s, a_s=a_s, s_sc=s_sc, s_tr=s_tr, s_gc=s_gc)

    def finish():
        if dbg:
            s_dbg = Sem(nc, "s_dbg")
            srcs = {"qf": qf_s, "kf": kf_s, "v": v_s, "y": y_s,
                    "attn": attn_s, "h1": h1f_s, "a": a_s}
            w = [(s.h, s.n) for s in (s_sc, s_tr, s_gc) if s.n]
            for nm in dbg:
                def f(e, out=dbg_outs[nm], in_=srcs[nm], w=tuple(w)):
                    for (h, v) in w:
                        e.wait_ge(h, v)
                    e.dma_start(out=out, in_=in_).then_inc(s_dbg.h, 16)
                P.on("gpsimd", f)
                s_dbg.bump(16)
            def fin(e, t=s_dbg.n):
                e.wait_ge(s_dbg.h, t)
            P.on("gpsimd", fin)
        P.realize(nc)
        return nc, dbg_outs

    op("vector", "memset", ones512[:], 1.0, inc=(s_dve.h, 1)); s_dve.bump()
    op("vector", "memset", onescol[:], 1.0, inc=(s_dve.h, 1))
    s_dve.bump()
    op("vector", "memset", epsrow[:], Z_EPS, inc=(s_dve.h, 1))
    t_consts = s_dve.bump()
    op("sync", "dma_start", out=maskt[:], in_=mask_d, inc=(s_tr.h, 16))
    t_mask = s_tr.bump(16)

    for (dst, src) in ((bq_sb, bq_d), (bk_sb, bk_d), (bv_sb, bv_d), (bo_sb, bo_d)):
        op("gpsimd", "dma_start", out=dst[:], in_=src, inc=(s_gc.h, 16))
        s_gc.bump(16)
    t_bias = s_gc.n

    t_xc = np.zeros((SEQ, 4), dtype=np.int64)
    _xc_jobs = [(seq, ch) for seq in range(SEQ) for ch in range(4)]
    _xc_i = {"i": 0}

    def emit_xc(n):
        for _ in range(n):
            i = _xc_i["i"]
            if i >= len(_xc_jobs):
                return
            seq, ch = _xc_jobs[i]
            op("gpsimd", "dma_start", out=x_bf[seq, ch * 512:(ch + 1) * 512, :],
               in_=x_in[seq, ch * 512:(ch + 1) * 512, :], inc=(s_gc.h, 16))
            t_xc[seq, ch] = s_gc.bump(16)
            _xc_i["i"] += 1

    emit_xc(2)

    _wkv_jobs = [(dst, srcd, ic) for (dst, srcd) in ((wk_sb, wk_d), (wv_sb, wv_d))
                 for ic in range(16)]
    _wkv_i = {"i": 0}

    def emit_wkv(n):
        for _ in range(n):
            i = _wkv_i["i"]
            if i >= len(_wkv_jobs):
                return
            dst, srcd, ic = _wkv_jobs[i]
            op("gpsimd", "dma_start", out=dst[:, ic * 2048:(ic + 1) * 2048],
               in_=srcd[ic * 128:(ic + 1) * 128, :], inc=(s_gc.h, 16))
            s_gc.bump(16)
            _wkv_i["i"] += 1

    wgu_T = [0] * 32
    wd_T = [0] * 8
    _wc = {"g": 0, "d": 0}

    def emit_wcast(n):
        for _ in range(n):
            if _wc["g"] < 32:
                g = _wc["g"]
                op("gpsimd", "dma_start", out=wgu_bf[g * 512:(g + 1) * 512, :],
                   in_=wgu_d[g * 512:(g + 1) * 512, :], inc=(s_gc.h, 16))
                wgu_T[g] = s_gc.bump(16)
                _wc["g"] += 1
            elif _wc["d"] < 8:
                d = _wc["d"]
                op("gpsimd", "dma_start", out=wd_bf[d * 256:(d + 1) * 256, :],
                   in_=wd_d[d * 256:(d + 1) * 256, :], inc=(s_gc.h, 16))
                wd_T[d] = s_gc.bump(16)
                _wc["d"] += 1

    wq_free = [(s_mm.h, 0), (s_mm.h, 0)]
    qpan_T = np.zeros((SEQ, 2, NO), dtype=np.int64)
    kvpan_T = np.zeros((SEQ,), dtype=np.int64)
    t_grp_prev = 0
    qstg_free = (s_sc.h, 0)
    kstg_free = (s_sc.h, 0)
    vstg_free = (s_sc.h, 0)
    tmin_free = (s_act.h, 0)
    texp_free = (s_dve.h, 0)

    _wkv_done = [0]
    for seq in range(SEQ):
        for tg in range(2):
            xT_w = [(s_gc.h, int(t_xc[seq, 2 * tg + 1]))]
            if t_grp_prev:
                xT_w.append((s_mm.h, t_grp_prev))
            for i in range(16):
                op("sync", "dma_start", out=xT[:, i * 1024:(i + 1) * 1024],
                   in_=x_bf[seq, tg * 1024:(tg + 1) * 1024, i * 128:(i + 1) * 128],
                   transpose=True, w=(xT_w if i == 0 else ()), inc=(s_tr.h, 16))
                s_tr.bump(16)
            t_xT = s_tr.n
            t_xT2 = 0

            for o in range(NO):
                p = o % 2
                op("gpsimd", "dma_start", out=wq_c[p][:],
                   in_=wq_d[:, o * 128:(o + 1) * 128].rearrange("(ic p) o -> p ic o", p=128),
                   w=[wq_free[p]], inc=(s_gc.h, 16))
                t_wq = s_gc.bump(16)
                if seq == 0 and tg == 0:
                    if o < 6:
                        emit_xc(1)
                    emit_wkv(2)
                    if o == NO - 1:
                        _wkv_done[0] = s_gc.n

                banks = [2 * p, 2 * p + 1]
                mms = []
                for i in range(16):
                    for tt in range(2):
                        mms.append((psb[banks[tt]][:, :],
                                    wq_c[p][:, i * 128:(i + 1) * 128],
                                    xT[:, i * 1024 + tt * 512:i * 1024 + (tt + 1) * 512],
                                    i == 0, False))
                for tt in range(2):
                    mms.append((psb[banks[tt]][:, :],
                                bq_sb[0:1, o * 128:(o + 1) * 128],
                                ones512[0:1, 0:512], False, True))
                w = [(s_gc.h, t_wq), (s_tr.h, t_xT), (s_sc.h, t_xT2),
                     ps_free[banks[0]], ps_free[banks[1]]]
                if seq == 0 and tg == 0 and o == 0:
                    w += [(s_dve.h, t_consts), (s_gc.h, t_bias)]
                t_mms(mms, w=w, inc=(s_mm.h, 1))
                t_q = s_mm.bump()
                wq_free[p] = (s_mm.h, t_q)

                for tt in range(2):
                    b = banks[tt]
                    op("vector", "tensor_scalar_min", tmin[:], psb[b][:, :], 0.0,
                       w=[(s_mm.h, t_q), tmin_free], inc=(s_dve.h, 1))
                    t_min = s_dve.bump()
                    op("scalar", "activation", texp[:], tmin[:], AF.Exp,
                       w=[(s_dve.h, t_min), texp_free], inc=(s_act.h, 1))
                    t_exp = s_act.bump()
                    tmin_free = (s_act.h, t_exp)
                    op("vector", "scalar_tensor_tensor",
                       qstg[:, tt * 512:(tt + 1) * 512], psb[b][:, :], 0.0, texp[:],
                       op0=ALU.max, op1=ALU.add,
                       w=[(s_act.h, t_exp)] + ([qstg_free] if tt == 0 else []),
                       inc=(s_dve.h, 1))
                    t_stt = s_dve.bump()
                    texp_free = (s_dve.h, t_stt)
                    ps_free[b] = (s_dve.h, t_stt)
                op("scalar", "dma_start",
                   out=qf_s[seq, o, :, tg * 1024:(tg + 1) * 1024], in_=qstg[:],
                   w=[(s_dve.h, t_stt)], inc=(s_sc.h, 16))
                qpan_T[seq, tg, o] = s_sc.bump(16)
                qstg_free = (s_sc.h, int(qpan_T[seq, tg, o]))

            for cs in range(8):
                c0 = tg * 1024 + cs * 128
                xsl = lambda i: xT[:, i * 1024 + cs * 128:i * 1024 + cs * 128 + 128]
                mms = []
                for i in range(16):
                    for eg in range(4):
                        mms.append((psb[eg][:, :], xsl(i),
                                    wk_sb[:, i * 2048 + eg * 512:i * 2048 + (eg + 1) * 512],
                                    i == 0, False))
                    for eg in range(4):
                        mms.append((psb[4 + eg][:, :], xsl(i),
                                    wv_sb[:, i * 2048 + eg * 512:i * 2048 + (eg + 1) * 512],
                                    i == 0, False))
                for eg in range(4):
                    mms.append((psb[eg][:, :], ones512[0:1, 0:128],
                                bk_sb[0:1, eg * 512:(eg + 1) * 512], False, True))
                for eg in range(4):
                    mms.append((psb[4 + eg][:, :], ones512[0:1, 0:128],
                                bv_sb[0:1, eg * 512:(eg + 1) * 512], False, True))
                w = [ps_free[b] for b in range(8)] + [(s_gc.h, _wkv_done[0])]
                t_mms(mms, w=w, inc=(s_mm.h, 1))
                t_kv = s_mm.bump()

                for eg in range(4):
                    op("vector", "tensor_scalar_min", tmin[:], psb[eg][:, :], 0.0,
                       w=[(s_mm.h, t_kv), tmin_free], inc=(s_dve.h, 1))
                    t_min = s_dve.bump()
                    op("scalar", "activation", texp[:], tmin[:], AF.Exp,
                       w=[(s_dve.h, t_min), texp_free], inc=(s_act.h, 1))
                    t_exp = s_act.bump()
                    tmin_free = (s_act.h, t_exp)
                    op("vector", "scalar_tensor_tensor",
                       kstg[:, eg * 512:(eg + 1) * 512], psb[eg][:, :], 0.0, texp[:],
                       op0=ALU.max, op1=ALU.add,
                       w=[(s_act.h, t_exp)] + ([kstg_free] if eg == 0 else []),
                       inc=(s_dve.h, 1))
                    t_stt = s_dve.bump()
                    texp_free = (s_dve.h, t_stt)
                    ps_free[eg] = (s_dve.h, t_stt)
                    op("scalar", "activation", vstg[:, eg * 512:(eg + 1) * 512],
                       psb[4 + eg][:, :], AF.Copy,
                       w=[(s_mm.h, t_kv)] + ([vstg_free] if eg == 0 else []),
                       inc=(s_act.h, 1))
                    t_vc = s_act.bump()
                    ps_free[4 + eg] = (s_act.h, t_vc)
                op("scalar", "dma_start",
                   out=kf_s[seq, :, c0:c0 + 128, :].rearrange("h c e -> c h e"),
                   in_=kstg[:], w=[(s_dve.h, t_stt)], inc=(s_sc.h, 16))
                t_kw = s_sc.bump(16)
                kstg_free = (s_sc.h, t_kw)
                op("scalar", "dma_start",
                   out=v_s[seq, :, c0:c0 + 128, :].rearrange("h c e -> c h e"),
                   in_=vstg[:], w=[(s_act.h, t_vc)], inc=(s_sc.h, 16))
                t_vw = s_sc.bump(16)
                vstg_free = (s_sc.h, t_vw)
            t_grp_prev = s_mm.n
            emit_wcast(2)
        kvpan_T[seq] = s_sc.n

    if upto < 3:
        return finish()

    yst_T = np.zeros((SEQ,), dtype=np.int64)
    ystg_free = {0: (s_tr.h, 0), 1: (s_tr.h, 0)}
    p12_mm_end = s_mm.n
    tile_free = [(s_mm.h, p12_mm_end), (s_mm.h, p12_mm_end)]
    pair_mm_end = [p12_mm_end, p12_mm_end]
    trk = {}
    for seq in range(SEQ):
        pan_ready = int(max(kvpan_T[seq], qpan_T[seq].max()))

        if seq == 0:
            for par in range(2):
                op("vector", "memset",
                   v_h[par][:].rearrange("c (n e) -> c n e", e=129)[:, :, 128:129],
                   1.0, w=[(s_mm.h, p12_mm_end)], inc=(s_dve.h, 1))
            t_vones = s_dve.bump(2)

        for hp in range(8):
            mm_guard = s_mm.n
            dve_guard = (s_dve.h, s_dve.n)
            act_guard = (s_act.h, s_act.n)
            for hh in range(2):
                h = hp * 2 + hh
                par = hh
                op("scalar", "dma_start", out=qf_h[par][:], in_=qf_s[seq, h, :, :],
                   w=[tile_free[par], (s_sc.h, pan_ready)], inc=(s_sc.h, 16))
                s_sc.bump(16)
                op("scalar", "dma_start", out=kftm_h[par][:],
                   in_=kf_s[seq, h, :, :].rearrange("(n c) e -> c n e", c=128),
                   inc=(s_sc.h, 16))
                s_sc.bump(16)
                op("scalar", "dma_start",
                   out=v_h[par][:].rearrange("c (n e) -> c n e", e=129)[:, :, 0:128],
                   in_=v_s[seq, h, :, :].rearrange("(n c) e -> c n e", c=128),
                   inc=(s_sc.h, 16))
                ld_T = s_sc.bump(16)
                op("sync", "dma_start", out=kffm_h[par][:], in_=kf_s[seq, h, :, :],
                   transpose=True, w=[tile_free[par]], inc=(s_tr.h, 16))
                tr_T = s_tr.bump(16)

                op("vector", "memset", kvbf[par][:], 0.0,
                   w=[(s_mm.h, pair_mm_end[par])], inc=(s_dve.h, 1))
                t_state0 = s_dve.bump()

                trk[par] = dict(
                    kv_copy=(s_dve.h, t_state0),
                    mask=dve_guard,
                    amask_rd=(s_mm.h, mm_guard),
                    y_free={0: dve_guard, 1: dve_guard},
                    kvupd=(s_mm.h, mm_guard),
                    y_done=(s_mm.h, mm_guard),
                    ld=ld_T, tr=tr_T,
                )

            for ch in range(NCH):
                csl = slice(ch * 128, (ch + 1) * 128)
                vsl = slice(ch * 129, (ch + 1) * 129)
                tA, tY = {}, {}
                if ch > 0:
                    for hh in range(2):
                        tk = trk[hh]
                        op("scalar", "activation", kvbf[hh][:], psb[0 + hh][:, 0:129],
                           AF.Copy, w=[tk["kvupd"]], inc=(s_act.h, 1))
                        tk["kv_copy"] = (s_act.h, s_act.bump())
                for hh in range(2):
                    tk = trk[hh]
                    wA = [tk["mask"]]
                    if ch == 0:
                        wA += [(s_sc.h, tk["ld"]), (s_tr.h, tk["tr"]),
                               (s_tr.h, t_mask), act_guard, (s_dve.h, t_vones)]
                    t_mms([(psb[2 + hh][:, 0:128], kffm_h[hh][:, csl],
                            qf_h[hh][:, csl], True, True)], w=wA, inc=(s_mm.h, 1))
                    tA[hh] = s_mm.bump()
                for hh in range(2):
                    tk = trk[hh]
                    op("vector", "tensor_tensor", amask[hh][:], psb[2 + hh][:, 0:128],
                       maskt[:], op=ALU.mult,
                       w=[(s_mm.h, tA[hh]), tk["amask_rd"]], inc=(s_dve.h, 1))
                    tk["mask"] = (s_dve.h, s_dve.bump())
                for hh in range(2):
                    tk = trk[hh]
                    bY = 4 + 2 * hh + (ch % 2)
                    wy = [tk["mask"], tk["kv_copy"], tk["y_free"][ch % 2]]
                    if ch == 0:
                        wy.append(act_guard)
                    t_mms([
                        (psb[bY][:, 0:129], qf_h[hh][:, csl], kvbf[hh][:, 0:129],
                         True, False),
                        (psb[bY][:, 0:129], amask[hh][:], v_h[hh][:, vsl], False, False),
                        (psb[bY][:, 128:129], epsrow[0:1, :], ones512[0:1, 0:1],
                         False, True),
                    ], w=wy, inc=(s_mm.h, 1))
                    tY[hh] = s_mm.bump()
                    tk["amask_rd"] = (s_mm.h, tY[hh])
                    tk["y_done"] = (s_mm.h, tY[hh])
                for hh in range(2):
                    tk = trk[hh]
                    bY = 4 + 2 * hh + (ch % 2)
                    op("vector", "reciprocal", zinv[hh][:], psb[bY][:, 128:129],
                       w=[(s_mm.h, tY[hh])], inc=(s_dve.h, 1))
                    t_zi = s_dve.bump()
                    op("vector", "tensor_scalar_mul",
                       ystg[ch % 2][:, hh * 128:(hh + 1) * 128], psb[bY][:, 0:128],
                       zinv[hh][:, 0:1],
                       w=[(s_dve.h, t_zi), ystg_free[ch % 2]],
                       inc=(s_dve.h, 1))
                    tk["y_free"][ch % 2] = (s_dve.h, s_dve.bump())
                if ch < NCH - 1:
                    for hh in range(2):
                        tk = trk[hh]
                        wup = [tk["kv_copy"]]
                        if ch == 0:
                            wup += [act_guard, dve_guard]
                        t_mms([(psb[0 + hh][:, 0:129], kftm_h[hh][:, csl],
                                v_h[hh][:, vsl], ch == 0, ch == NCH - 2)],
                              w=wup, inc=(s_mm.h, 1))
                        tk["kvupd"] = (s_mm.h, s_mm.bump())

                op("sync", "dma_start",
                   out=y_s[seq, ch * 128:(ch + 1) * 128, hp * 256:(hp + 1) * 256],
                   in_=ystg[ch % 2][:],
                   w=[trk[0]["y_free"][ch % 2], trk[1]["y_free"][ch % 2]],
                   inc=(s_tr.h, 16))
                ystg_free[ch % 2] = (s_tr.h, s_tr.bump(16))

            for hh in range(2):
                tile_free[hh] = (s_mm.h, s_mm.n)
                pair_mm_end[hh] = s_mm.n
        yst_T[seq] = s_tr.n
        emit_wcast(10)

    p3_dve = (s_dve.h, s_dve.n)
    p3_act = (s_act.h, s_act.n)
    for b in range(8):
        ps_free[b] = p3_act if b < 4 else p3_dve
    ps_free[4] = ps_free[5] = p3_dve
    ps_free[6] = ps_free[7] = p3_dve
    ps_free[0] = ps_free[1] = p3_act
    ps_free[2] = ps_free[3] = p3_act

    if upto < 4:
        return finish()

    wo_free = [(s_mm.h, p12_mm_end), (s_mm.h, p12_mm_end)]
    attnst_T = np.zeros((SEQ,), dtype=np.int64)
    h1b_T = np.zeros((SEQ,), dtype=np.int64)
    h1f_T = np.zeros((SEQ, 16), dtype=np.int64)
    yT_loaded_mm = 0
    attnstg_free = (s_sc.h, 0)
    ln_guard = [(s_dve.h, 0)]
    ln_store_guard = []

    def emit_ln(in_bf_tile, in_f32_tile, w_in, out_f32, out_b16, stores):
        nonlocal ln_guard, ln_store_guard
        op("vector", "tensor_add", r1[:], in_bf_tile[:], in_f32_tile[:],
           w=list(w_in) + ln_guard, inc=(s_dve.h, 1))
        t_r = s_dve.bump()
        op("vector", "reduce_sum", lnst[:, 0:1], r1[:], axis=AX.X,
           w=[(s_dve.h, t_r)], inc=(s_dve.h, 1))
        t_s0 = s_dve.bump()
        op("scalar", "activation", sqt[:], r1[:], AF.Square,
           w=[(s_dve.h, t_r)], inc=(s_act.h, 1))
        t_sq = s_act.bump()
        op("vector", "reduce_sum", lnst[:, 1:2], sqt[:], axis=AX.X,
           w=[(s_act.h, t_sq), (s_dve.h, t_s0)], inc=(s_dve.h, 1))
        t_s1 = s_dve.bump()
        op("vector", "tensor_scalar_mul", lnst[:, 2:3], lnst[:, 0:1], 1.0 / DM,
           w=[(s_dve.h, t_s1)], inc=(s_dve.h, 1))
        t_m = s_dve.bump()
        op("vector", "tensor_scalar_mul", lnst[:, 3:4], lnst[:, 1:2], 1.0 / DM,
           w=[(s_dve.h, t_m)], inc=(s_dve.h, 1))
        t_m2 = s_dve.bump()
        op("vector", "tensor_tensor", lnst[:, 4:5], lnst[:, 2:3], lnst[:, 2:3],
           op=ALU.mult, w=[(s_dve.h, t_m2)], inc=(s_dve.h, 1))
        t_mm2 = s_dve.bump()
        op("vector", "tensor_sub", lnst[:, 5:6], lnst[:, 3:4], lnst[:, 4:5],
           w=[(s_dve.h, t_mm2)], inc=(s_dve.h, 1))
        t_v = s_dve.bump()
        op("vector", "tensor_scalar_add", lnst[:, 5:6], lnst[:, 5:6], LN_EPS,
           w=[(s_dve.h, t_v)], inc=(s_dve.h, 1))
        t_ve = s_dve.bump()
        op("scalar", "activation", lnst[:, 6:7], lnst[:, 5:6], AF.Sqrt,
           w=[(s_dve.h, t_ve)], inc=(s_act.h, 1))
        t_sd = s_act.bump()
        op("vector", "reciprocal", lnst[:, 7:8], lnst[:, 6:7],
           w=[(s_act.h, t_sd)], inc=(s_dve.h, 1))
        t_rs = s_dve.bump()
        op("vector", "tensor_scalar_mul", lnst[:, 8:9], lnst[:, 2:3], -1.0,
           w=[(s_dve.h, t_rs)], inc=(s_dve.h, 1))
        t_nm = s_dve.bump()
        op("vector", "tensor_scalar", out_f32[:], r1[:], lnst[:, 8:9], lnst[:, 7:8],
           op0=ALU.add, op1=ALU.mult,
           w=[(s_dve.h, t_nm)] + ln_store_guard, inc=(s_dve.h, 1))
        t_norm = s_dve.bump()
        t_bf = None
        if out_b16 is not None:
            op("vector", "tensor_copy", out_b16[:], out_f32[:],
               w=[(s_dve.h, t_norm)], inc=(s_dve.h, 1))
            t_bf = s_dve.bump()
        ln_guard = [(s_dve.h, s_dve.n)]
        ln_store_guard = stores(t_norm, t_bf)

    for seq in range(SEQ):
        for E in range(16):
            w = [(s_tr.h, int(yst_T[seq]))]
            if E == 0 and seq > 0:
                w.append((s_mm.h, yT_loaded_mm))
            op("sync", "dma_start", out=yT[:, E * 2048:(E + 1) * 2048],
               in_=y_s[seq, :, E * 128:(E + 1) * 128], transpose=True,
               w=w, inc=(s_tr.h, 16))
            s_tr.bump(16)
        t_yT = s_tr.n
        t_yT2 = 0

        for o in range(NO):
            p = o % 2
            op("gpsimd", "dma_start", out=wo_c[p][:],
               in_=wo_d[:, o * 128:(o + 1) * 128].rearrange("(ic p) o -> p ic o", p=128),
               w=[wo_free[p]], inc=(s_gc.h, 16))
            t_wo = s_gc.bump(16)

            banks = [4 * p + tt for tt in range(4)]
            mms = []
            for E in range(16):
                for tt in range(4):
                    mms.append((psb[banks[tt]][:, :],
                                wo_c[p][:, E * 128:(E + 1) * 128],
                                yT[:, E * 2048 + tt * 512:E * 2048 + (tt + 1) * 512],
                                E == 0, False))
            for tt in range(4):
                mms.append((psb[banks[tt]][:, :], bo_sb[0:1, o * 128:(o + 1) * 128],
                            ones512[0:1, 0:512], False, True))
            w = [(s_gc.h, t_wo), (s_tr.h, t_yT), (s_sc.h, t_yT2)] + [ps_free[b] for b in banks]
            t_mms(mms, w=w, inc=(s_mm.h, 1))
            t_woc = s_mm.bump()
            wo_free[p] = (s_mm.h, t_woc)

            for tt in range(4):
                b = banks[tt]
                op("scalar", "activation", attnstg[:, tt * 512:(tt + 1) * 512],
                   psb[b][:, :], AF.Copy,
                   w=[(s_mm.h, t_woc)] + ([attnstg_free] if tt == 0 else []),
                   inc=(s_act.h, 1))
                t_ev = s_act.bump()
                ps_free[b] = (s_act.h, t_ev)
            op("scalar", "dma_start",
               out=attn_s[o * 128:(o + 1) * 128, seq * 2048:(seq + 1) * 2048],
               in_=attnstg[:], w=[(s_act.h, t_ev)], inc=(s_sc.h, 16))
            attnstg_free = (s_sc.h, s_sc.bump(16))
        attnst_T[seq] = s_sc.n
        yT_loaded_mm = s_mm.n

        attn_tm_free = [(s_dve.h, 0)]
        for tc in range(16):
            tglob = seq * 2048 + tc * 128
            op("sync", "dma_start", out=attn_tm[:],
               in_=attn_s[:, tglob:tglob + 128], transpose=True,
               w=[(s_sc.h, int(attnst_T[seq]))] + attn_tm_free, inc=(s_tr.h, 16))
            t_at = s_tr.bump(16)
            op("scalar", "dma_start", out=x_tm[:],
               in_=x_in[seq, tc * 128:(tc + 1) * 128, :],
               w=attn_tm_free, inc=(s_sc.h, 16))
            t_xt = s_sc.bump(16)

            def stores(t_norm, t_bf, seq=seq, tc=tc, tglob=tglob):
                op("scalar", "dma_start", out=h1f_s[tglob:tglob + 128, :], in_=h1f[:],
                   w=[(s_dve.h, t_norm)], inc=(s_sc.h, 16))
                h1f_T[seq, tc] = s_sc.bump(16)
                op("scalar", "dma_start", out=h1b_s[tglob:tglob + 128, :], in_=h1b[:],
                   w=[(s_dve.h, t_bf)], inc=(s_sc.h, 16))
                return [(s_sc.h, s_sc.bump(16))]

            emit_ln(attn_tm, x_tm, [(s_tr.h, t_at), (s_sc.h, t_xt)], h1f, h1b, stores)
            attn_tm_free = [(s_dve.h, s_dve.n)]
        h1b_T[seq] = s_sc.n
        emit_wcast(6)
        emit_wcast(0)

    t_p4_mm_end = s_mm.n

    if upto < 5:
        return finish()

    emit_wcast(48)
    gate_T = np.zeros((64,), dtype=np.int64)
    a_T = np.zeros((64,), dtype=np.int64)
    wt_free = [(s_mm.h, t_p4_mm_end), (s_mm.h, t_p4_mm_end)]
    gstg_free = (s_sc.h, 0)
    ustg_free = (s_sc.h, 0)
    grd_free = (s_dve.h, 0)
    h1T_T = [0, 0]
    wdt_T = 0
    _wdt_jobs = [(og, jc) for og in range(4) for jc in range(64)]
    _wdt_i = {"i": 0}
    wdtr_free = [(s_gc.h, 0)] * 4
    wdt_T2 = 0

    def emit_wdt(n):
        nonlocal wdt_T, wdt_T2
        for _ in range(n):
            i = _wdt_i["i"]
            if i >= len(_wdt_jobs):
                return
            og, jc = _wdt_jobs[i]
            p = i % 4
            op("sync", "dma_start", out=wdtr[p][:],
               in_=wd_bf[og * 512:(og + 1) * 512, jc * 128:(jc + 1) * 128],
               transpose=True,
               w=[(s_gc.h, wd_T[og * 2 + 1]), wdtr_free[p],
                  (s_mm.h, t_p4_mm_end if i < 4 else 0)],
               inc=(s_tr.h, 16))
            t_t = s_tr.bump(16)
            op("gpsimd", "dma_start",
               out=wdt_s[og * 4:(og + 1) * 4, jc, :, :].rearrange("oc j o -> j oc o"),
               in_=wdtr[p][:], w=[(s_tr.h, t_t)], inc=(s_gc.h, 16))
            wdt_T2 = s_gc.bump(16)
            wdtr_free[p] = (s_gc.h, wdt_T2)
            wdt_T = wdt_T2
            _wdt_i["i"] += 1

    for th in range(2):
        for i in range(16):
            w = [(s_sc.h, int(h1b_T[th]))]
            if i == 0:
                w.append((s_mm.h, t_p4_mm_end if th == 0 else s_mm.n))
            op("sync", "dma_start", out=h1T[:, i * 2048:(i + 1) * 2048],
               in_=h1b_s[th * 2048:(th + 1) * 2048, i * 128:(i + 1) * 128],
               transpose=True, w=w, inc=(s_tr.h, 16))
            s_tr.bump(16)
        h1T_T[th] = s_tr.n
        h1T_T2 = 0

        for phase in range(2):
            for gg in range(16):
                g = phase * 16 + gg
                p = gg % 2
                for i in range(16):
                    w = [(s_gc.h, wgu_T[g])]
                    if i == 0:
                        w.append(wt_free[p])
                        if th == 0 and phase == 0 and gg == 0:
                            w.append((s_mm.h, t_p4_mm_end))
                    op("sync", "dma_start", out=wt5[p][:, i * 512:(i + 1) * 512],
                       in_=wgu_bf[g * 512:(g + 1) * 512, i * 128:(i + 1) * 128],
                       transpose=True, w=w, inc=(s_tr.h, 16))
                    s_tr.bump(16)
                t_wt = s_tr.n
                t_wt2 = 0

                for js in range(4):
                    jc = g * 4 + js
                    banks = [(js % 2) * 4 + tt for tt in range(4)]
                    mms = []
                    for i in range(16):
                        for tt in range(4):
                            mms.append((psb[banks[tt]][:, :],
                                        wt5[p][:, i * 512 + js * 128:i * 512 + (js + 1) * 128],
                                        h1T[:, i * 2048 + tt * 512:i * 2048 + (tt + 1) * 512],
                                        i == 0, i == 15))
                    w = [ps_free[b] for b in banks]
                    if js == 0:
                        w += [(s_tr.h, t_wt), (s_sc.h, t_wt2),
                              (s_tr.h, h1T_T[th]), (s_sc.h, h1T_T2)]
                    t_mms(mms, w=w, inc=(s_mm.h, 1))
                    t_g = s_mm.bump()
                    if js == 3:
                        wt_free[p] = (s_mm.h, t_g)

                    if phase == 0:
                        for tt in range(4):
                            b = banks[tt]
                            op("scalar", "activation", gstg[:, tt * 512:(tt + 1) * 512],
                               psb[b][:, :], AF.Silu,
                               w=[(s_mm.h, t_g)] + ([gstg_free] if tt == 0 else []),
                               inc=(s_act.h, 1))
                            t_ev = s_act.bump()
                            ps_free[b] = (s_act.h, t_ev)
                        op("scalar", "dma_start",
                           out=gate_s[jc, :, th * 2048:(th + 1) * 2048],
                           in_=gstg[:], w=[(s_act.h, t_ev)], inc=(s_sc.h, 16))
                        gate_T[jc] = s_sc.bump(16)
                        gstg_free = (s_sc.h, int(gate_T[jc]))
                    else:
                        jcu = jc - 64
                        op("scalar", "dma_start", out=grd[:],
                           in_=gate_s[jcu, :, th * 2048:(th + 1) * 2048],
                           w=[(s_sc.h, int(gate_T[jcu])), grd_free], inc=(s_sc.h, 16))
                        t_gr = s_sc.bump(16)
                        for tt in range(4):
                            b = banks[tt]
                            op("vector", "tensor_tensor",
                               ustg[:, tt * 512:(tt + 1) * 512], psb[b][:, :],
                               grd[:, tt * 512:(tt + 1) * 512], op=ALU.mult,
                               w=[(s_mm.h, t_g), (s_sc.h, t_gr)] +
                                 ([ustg_free] if tt == 0 else []),
                               inc=(s_dve.h, 1))
                            t_ev = s_dve.bump()
                            ps_free[b] = (s_dve.h, t_ev)
                        grd_free = (s_dve.h, t_ev)
                        op("scalar", "dma_start",
                           out=a_s[jcu, :, th * 2048:(th + 1) * 2048],
                           in_=ustg[:], w=[(s_dve.h, t_ev)], inc=(s_sc.h, 16))
                        a_T[jcu] = s_sc.bump(16)
                        ustg_free = (s_sc.h, int(a_T[jcu]))
                emit_wdt(4)

    t_p5_mm_end = s_mm.n
    emit_wdt(256)
    t_a_all = int(a_T.max())

    if upto < 6:
        return finish()

    wdt6_free = [(s_mm.h, 0), (s_mm.h, 0)]
    ffn_T = np.zeros((8,), dtype=np.int64)
    pair_mm6 = [0] * 4
    ffn_tm_free = [(s_dve.h, 0)]
    dstg_free = [(s_sc.h, 0), (s_sc.h, 0)]

    ap_T = {}

    def emit_apan(tp):
        for part in range(2):
            for half in range(2):
                tg = 2 * tp + half
                for q4 in range(4):
                    q8 = part * 4 + q4
                    w = []
                    if part == 0 and half == 0 and q4 == 0:
                        w = [(s_sc.h, t_a_all), (s_gc.h, wdt_T),
                             (s_mm.h, t_p5_mm_end if tp == 0 else pair_mm6[tp - 1])]
                    op("sync", "dma_start",
                       out=apan[half][:, q8 * 8 * 512:(q8 + 1) * 8 * 512],
                       in_=a_s[q8 * 8:(q8 + 1) * 8, :, tg * 512:(tg + 1) * 512]
                           .rearrange("jc j t -> j jc t"),
                       w=w, inc=(s_tr.h, 16))
                    s_tr.bump(16)
            if part == 0:
                t_lo = s_tr.n
        ap_T[tp] = (t_lo, s_tr.n)

    emit_apan(0)
    _ln2_jobs = []

    def emit_ln2(n):
        for _ in range(n):
            if not _ln2_jobs:
                return
            _ln2_jobs.pop(0)()

    for tp in range(4):
        t_ap_lo, t_ap_hi = ap_T[tp]

        for o in range(NO):
            bb = [(o % 2) * 2, (o % 2) * 2 + 1]
            t_d = None
            for hf in range(2):
                ph = (o * 2 + hf) % 2
                qn, sem = (("scalar", s_sc) if hf == 0 else ("sync", s_tr))
                op(qn, "dma_start", out=wdt6[ph][:],
                   in_=wdt_s[o, hf * 32:(hf + 1) * 32, :, :].rearrange("jc j o -> j jc o"),
                   w=[(s_gc.h, wdt_T), wdt6_free[ph]], inc=(sem.h, 16))
                t_wl = (sem.h, sem.bump(16))

                mms = []
                for jl in range(32):
                    jc = hf * 32 + jl
                    for half in range(2):
                        mms.append((psb[bb[half]][:, :],
                                    wdt6[ph][:, jl * 128:(jl + 1) * 128],
                                    apan[half][:, jc * 512:(jc + 1) * 512],
                                    jc == 0, jc == 63))
                w = [t_wl]
                if hf == 0:
                    w += [ps_free[bb[0]], ps_free[bb[1]]]
                    if o == 0:
                        w.append((s_tr.h, t_ap_lo))
                elif o == 0:
                    w.append((s_tr.h, t_ap_hi))
                t_mms(mms, w=w, inc=(s_mm.h, 1))
                t_d = s_mm.bump()
                wdt6_free[ph] = (s_mm.h, t_d)

            for half in range(2):
                tg = 2 * tp + half
                op("scalar", "activation", dstg[half][:], psb[bb[half]][:, :], AF.Copy,
                   w=[(s_mm.h, t_d), dstg_free[half]], inc=(s_act.h, 1))
                t_ev = s_act.bump()
                ps_free[bb[half]] = (s_act.h, t_ev)
                op("scalar", "dma_start",
                   out=ffn_s[o * 128:(o + 1) * 128, tg * 512:(tg + 1) * 512],
                   in_=dstg[half][:], w=[(s_act.h, t_ev)], inc=(s_sc.h, 16))
                dstg_free[half] = (s_sc.h, s_sc.bump(16))
            emit_ln2(1)
        ffn_T[2 * tp] = ffn_T[2 * tp + 1] = s_sc.n
        pair_mm6[tp] = s_mm.n
        ffn_T_now = int(ffn_T[2 * tp])

        def mk_ln2(tglob, ffn_T_now=ffn_T_now):
            def job():
                nonlocal ffn_tm_free
                seq, trow = tglob // 2048, tglob % 2048
                op("sync", "dma_start", out=ffn_tm[:],
                   in_=ffn_s[:, tglob:tglob + 128], transpose=True,
                   w=[(s_sc.h, ffn_T_now)] + ffn_tm_free, inc=(s_tr.h, 16))
                t_ft = s_tr.bump(16)
                op("gpsimd", "dma_start", out=h1f_rd[:],
                   in_=h1f_s[tglob:tglob + 128, :],
                   w=[(s_sc.h, int(h1f_T[seq, trow // 128]))] + ffn_tm_free,
                   inc=(s_gc.h, 16))
                t_hr = s_gc.bump(16)

                def stores2(t_norm, t_bf, seq=seq, trow=trow):
                    op("sync", "dma_start", out=out_d[seq, trow:trow + 128, :],
                       in_=outstg[:], w=[(s_dve.h, t_norm)], inc=(s_out.h, 16))
                    return [(s_out.h, s_out.bump(16))]

                emit_ln(ffn_tm, h1f_rd, [(s_tr.h, t_ft), (s_gc.h, t_hr)],
                        outstg, None, stores2)
                ffn_tm_free = [(s_dve.h, s_dve.n)]
            return job

        for q4 in range(8):
            _ln2_jobs.append(mk_ln2(tp * 1024 + q4 * 128))
        if tp < 3:
            emit_apan(tp + 1)
        else:
            emit_ln2(64)

    emit_ln2(64)
    op("sync", "wait_ge", s_out.h, s_out.n)
    return finish()


def kernel(L0_attn_x, L0_attn_Wq, L0_attn_bq, L0_attn_Wk, L0_attn_bk,
           L0_attn_Wv, L0_attn_bv, L0_attn_Wo, L0_attn_bo,
           L0_ffn_w_gate_up, L0_ffn_w_down, _upto=6, _dbg=(), _cores=8):
    x = np.ascontiguousarray(np.asarray(L0_attn_x, dtype=np.float32))
    wq = np.ascontiguousarray(np.asarray(L0_attn_Wq, dtype=np.float32))
    wk = np.ascontiguousarray(np.asarray(L0_attn_Wk, dtype=np.float32))
    wv = np.ascontiguousarray(np.asarray(L0_attn_Wv, dtype=np.float32))
    wo = np.ascontiguousarray(np.asarray(L0_attn_Wo, dtype=np.float32))
    bq = np.ascontiguousarray(np.asarray(L0_attn_bq, dtype=np.float32).reshape(1, DM))
    bk = np.ascontiguousarray(np.asarray(L0_attn_bk, dtype=np.float32).reshape(1, DM))
    bv = np.ascontiguousarray(np.asarray(L0_attn_bv, dtype=np.float32).reshape(1, DM))
    bo = np.ascontiguousarray(np.asarray(L0_attn_bo, dtype=np.float32).reshape(1, DM))
    wgu = np.ascontiguousarray(np.asarray(L0_ffn_w_gate_up, dtype=np.float32))
    wd = np.ascontiguousarray(np.asarray(L0_ffn_w_down, dtype=np.float32))
    cmask = np.triu(np.ones((128, 128))).astype(ml_dtypes.bfloat16)

    nc, dbg_outs = build(upto=_upto, dbg=_dbg)

    in_maps = []
    for c in range(_cores):
        in_maps.append(dict(
            x=x[2 * c:2 * c + 2], wq=wq, wk=wk, wv=wv, wo=wo,
            bq=bq, bk=bk, bv=bv, bo=bo, wgu=wgu, wd=wd, cmask=cmask,
        ))
    res = run_bass_kernel_spmd(nc, in_maps, core_ids=list(range(_cores)))
    if _dbg:
        return res
    return np.concatenate([r["out"] for r in res.results], axis=0)


# version 10
# speedup vs baseline: 1.2340x; 1.0047x over previous
import numpy as np
import ml_dtypes
import concourse.bass as bass
import concourse.mybir as mybir
from concourse.bass_utils import run_bass_kernel_spmd

F32 = mybir.dt.float32
BF16 = mybir.dt.bfloat16
AF = mybir.ActivationFunctionType
ALU = mybir.AluOpType
AX = mybir.AxisListType

SEQ, S, DM = 2, 2048, 2048
H, DH, C, NCH = 16, 128, 128, 16
IN, JT = 8192, 16384
T = SEQ * S
NO = DM // 128
LN_EPS = 1e-5
Z_EPS = 1e-6


class Sem:
    def __init__(self, nc, name):
        self.h = nc.alloc_semaphore(name)
        self.n = 0

    def bump(self, k=1):
        self.n += k
        return self.n


class Prog:
    def __init__(self):
        self.items = []

    def on(self, eng, fn):
        self.items.append((eng, fn))

    def realize(self, nc):
        with nc.Block() as block:
            for name in ("sync", "gpsimd", "scalar", "vector", "tensor"):
                def mk(name):
                    def run(e):
                        for (n, fn) in self.items:
                            if n == name:
                                fn(e)
                    return run
                getattr(block, name)(mk(name))


def build(upto=6, dbg=()):
    nc = bass.Bass("TRN2", target_bir_lowering=False)

    x_in = nc.dram_tensor("x", [SEQ, S, DM], F32, kind="ExternalInput").ap()
    wq_d = nc.dram_tensor("wq", [DM, DM], F32, kind="ExternalInput").ap()
    wk_d = nc.dram_tensor("wk", [DM, DM], F32, kind="ExternalInput").ap()
    wv_d = nc.dram_tensor("wv", [DM, DM], F32, kind="ExternalInput").ap()
    wo_d = nc.dram_tensor("wo", [DM, DM], F32, kind="ExternalInput").ap()
    bq_d = nc.dram_tensor("bq", [1, DM], F32, kind="ExternalInput").ap()
    bk_d = nc.dram_tensor("bk", [1, DM], F32, kind="ExternalInput").ap()
    bv_d = nc.dram_tensor("bv", [1, DM], F32, kind="ExternalInput").ap()
    bo_d = nc.dram_tensor("bo", [1, DM], F32, kind="ExternalInput").ap()
    wgu_d = nc.dram_tensor("wgu", [JT, DM], F32, kind="ExternalInput").ap()
    wd_d = nc.dram_tensor("wd", [DM, IN], F32, kind="ExternalInput").ap()
    mask_d = nc.dram_tensor("cmask", [128, 128], BF16, kind="ExternalInput").ap()

    out_d = nc.dram_tensor("out", [SEQ, S, DM], F32, kind="ExternalOutput").ap()

    x_bf = nc.dram_tensor("x_bf", [SEQ, S, DM], BF16, kind="Internal").ap()
    qf_s = nc.dram_tensor("qf_s", [SEQ, H, DH, S], BF16, kind="Internal").ap()
    kf_s = nc.dram_tensor("kf_s", [SEQ, H, S, DH], BF16, kind="Internal").ap()
    v_s = nc.dram_tensor("v_s", [SEQ, H, S, DH], BF16, kind="Internal").ap()
    y_s = nc.dram_tensor("y_s", [SEQ, S, DM], BF16, kind="Internal").ap()
    attn_s = nc.dram_tensor("attn_s", [DM, T], BF16, kind="Internal").ap()
    h1f_s = nc.dram_tensor("h1f_s", [T, DM], F32, kind="Internal").ap()
    h1b_s = nc.dram_tensor("h1b_s", [T, DM], BF16, kind="Internal").ap()
    wgu_bf = nc.dram_tensor("wgu_bf", [JT, DM], BF16, kind="Internal").ap()
    wd_bf = nc.dram_tensor("wd_bf", [DM, IN], BF16, kind="Internal").ap()
    wdt_s = nc.dram_tensor("wdt_s", [NO, IN // 128, 128, 128], BF16, kind="Internal").ap()
    gate_s = nc.dram_tensor("gate_s", [IN // 128, 128, T], BF16, kind="Internal").ap()
    a_s = nc.dram_tensor("a_s", [IN // 128, 128, T], BF16, kind="Internal").ap()
    ffn_s = nc.dram_tensor("ffn_s", [DM, T], BF16, kind="Internal").ap()

    dbg_outs = {}
    dbg_specs = {"qf": ([SEQ, H, DH, S], BF16, qf_s), "kf": ([SEQ, H, S, DH], BF16, kf_s),
                 "v": ([SEQ, H, S, DH], BF16, v_s), "y": ([SEQ, S, DM], BF16, y_s),
                 "attn": ([DM, T], BF16, attn_s), "h1": ([T, DM], F32, h1f_s),
                 "a": ([IN // 128, 128, T], BF16, a_s)}
    for name in dbg:
        shp, dt, _src = dbg_specs[name]
        dbg_outs[name] = nc.dram_tensor("dbg_" + name, shp, dt, kind="ExternalOutput").ap()

    A = (nc.sbuf_base + 511) & ~511
    LIMIT = nc.sbuf_top

    def at(name, off, shape, dtype):
        assert (A + off) % 32 == 0, (name, off)
        return nc.alloc_sbuf_tensor_at(name, shape, dtype, offset=A + off)

    def fit(off_end):
        assert A + off_end <= LIMIT, f"SBUF overflow: A={A} +{off_end} > {LIMIT}"

    ones512 = at("ones512", 0, [1, 512], BF16)
    onescol = at("onescol", 1024, [128, 1], BF16)
    maskt = at("maskt", 1056, [128, 128], BF16)
    bq_sb = at("bq_sb", 1312, [1, DM], BF16)
    bk_sb = at("bk_sb", 5408, [1, DM], BF16)
    bv_sb = at("bv_sb", 9504, [1, DM], BF16)
    bo_sb = at("bo_sb", 13600, [1, DM], BF16)
    lnst = at("lnst", 17696, [128, 16], F32)
    epsrow = at("epsrow", 17760, [1, 128], BF16)
    B = 18048

    wk_sb = at("wk_sb", B + 0, [128, 16 * 2048], BF16)
    wv_sb = at("wv_sb", B + 65536, [128, 16 * 2048], BF16)
    xT = at("xT", B + 131072, [128, 16 * 1024], BF16)
    wq_c = [at(f"wq_c{p}", B + 163840 + p * 4096, [128, 16 * 128], BF16) for p in range(2)]
    qstg = at("qstg", B + 172032, [128, 1024], BF16)
    kstg = at("kstg", B + 174080, [128, 2048], BF16)
    vstg = at("vstg", B + 178176, [128, 2048], BF16)
    tmin = at("tmin", B + 182272, [128, 512], F32)
    texp = at("texp", B + 184320, [128, 512], F32)
    fit(B + 186368)

    qf_h = [at(f"qf_h{p}", B + p * 4096, [128, 2048], BF16) for p in range(2)]
    kftm_h = [at(f"kftm_h{p}", B + 8192 + p * 4096, [128, 2048], BF16) for p in range(2)]
    kffm_h = [at(f"kffm_h{p}", B + 16384 + p * 4096, [128, 2048], BF16) for p in range(2)]
    v_h = [at(f"v_h{p}", B + 24576 + p * 4160, [128, 16 * 129], BF16) for p in range(2)]
    kvbf = [at(f"kvbf{p}", B + 32896 + p * 512, [128, 129], BF16) for p in range(2)]
    amask = [at(f"amask{p}", B + 33920 + p * 256, [128, 128], BF16) for p in range(2)]
    zinv = [at(f"zinv{p}", B + 34432 + p * 32, [128, 1], F32) for p in range(2)]
    ystg = [at(f"ystg{p}", B + 34496 + p * 512, [128, 256], BF16) for p in range(2)]

    yT = at("yT", B + 40960, [128, 16 * 2048], BF16)
    wo_c = [at(f"wo_c{p}", B + 106496 + p * 4096, [128, 16 * 128], BF16) for p in range(2)]
    attnstg = at("attnstg", B + 114688, [128, 2048], BF16)
    wdtr = [at(f"wdtr{p}", B + 120320 + p * 1024, [128, 512], BF16) for p in range(4)]

    TOP = B + 131072
    r1 = at("r1", TOP, [128, 2048], F32)
    sqt = at("sqt", TOP + 8192, [128, 2048], F32)
    attn_tm = at("attn_tm", TOP + 16384, [128, 2048], BF16)
    ffn_tm = at("ffn_tm", TOP + 16384, [128, 2048], BF16)
    x_tm = at("x_tm", TOP + 20480, [128, 2048], F32)
    h1f_rd = at("h1f_rd", TOP + 20480, [128, 2048], F32)
    h1f = at("h1f", TOP + 28672, [128, 2048], F32)
    outstg = at("outstg", TOP + 28672, [128, 2048], F32)
    h1b = at("h1b", TOP + 36864, [128, 2048], BF16)
    dstg = [at(f"dstg{p}", TOP + 36864 + p * 1024, [128, 512], BF16) for p in range(2)]
    wdt6 = [at(f"wdt6{p}", TOP + 40960 + p * 8192, [128, 32 * 128], BF16) for p in range(2)]
    fit(TOP + 57344)

    h1T = at("h1T", B + 0, [128, 16 * 2048], BF16)
    wt5 = [at(f"wt5{p}", B + 65536 + p * 16384, [128, 16 * 512], BF16) for p in range(2)]
    gstg = at("gstg", B + 98304, [128, 2048], BF16)
    ustg = at("ustg", B + 102400, [128, 2048], BF16)
    grd = at("grd", B + 106496, [128, 2048], BF16)

    apan = [at(f"apan{p}", B + p * 65536, [128, 64 * 512], BF16) for p in range(2)]

    psb = [nc.alloc_psum_tensor(f"psb{b}", [128, 512], F32) for b in range(8)]

    P = Prog()
    s_gc = Sem(nc, "s_gc")
    s_tr = Sem(nc, "s_tr")
    s_sc = Sem(nc, "s_sc")
    s_mm = Sem(nc, "s_mm")
    s_dve = Sem(nc, "s_dve")
    s_act = Sem(nc, "s_act")
    s_out = Sem(nc, "s_out")

    def op(eng, method, *args, w=(), inc=None, **kw):
        def f(e, method=method, args=args, kw=kw, w=tuple(w), inc=inc):
            for (h, v) in w:
                if v > 0:
                    e.wait_ge(h, v)
            ins = getattr(e, method)(*args, **kw)
            if inc is not None:
                ins.then_inc(inc[0], inc[1])
        P.on(eng, f)

    def t_mms(mms, w=(), inc=None):
        def f(e, mms=tuple(mms), w=tuple(w), inc=inc):
            for (h, v) in w:
                if v > 0:
                    e.wait_ge(h, v)
            ins = None
            for (out, lhsT, rhs, st, sp) in mms:
                ins = e.matmul(out, lhsT, rhs, start=st, stop=sp)
            if inc is not None:
                ins.then_inc(inc[0], inc[1])
        P.on("tensor", f)

    ps_free = [(s_dve.h, 0)] * 8

    def env():
        return dict(qf_s=qf_s, kf_s=kf_s, v_s=v_s, y_s=y_s, attn_s=attn_s,
                    h1f_s=h1f_s, a_s=a_s, s_sc=s_sc, s_tr=s_tr, s_gc=s_gc)

    def finish():
        if dbg:
            s_dbg = Sem(nc, "s_dbg")
            srcs = {"qf": qf_s, "kf": kf_s, "v": v_s, "y": y_s,
                    "attn": attn_s, "h1": h1f_s, "a": a_s}
            w = [(s.h, s.n) for s in (s_sc, s_tr, s_gc) if s.n]
            for nm in dbg:
                def f(e, out=dbg_outs[nm], in_=srcs[nm], w=tuple(w)):
                    for (h, v) in w:
                        e.wait_ge(h, v)
                    e.dma_start(out=out, in_=in_).then_inc(s_dbg.h, 16)
                P.on("gpsimd", f)
                s_dbg.bump(16)
            def fin(e, t=s_dbg.n):
                e.wait_ge(s_dbg.h, t)
            P.on("gpsimd", fin)
        P.realize(nc)
        return nc, dbg_outs

    op("vector", "memset", ones512[:], 1.0, inc=(s_dve.h, 1)); s_dve.bump()
    op("vector", "memset", onescol[:], 1.0, inc=(s_dve.h, 1))
    s_dve.bump()
    op("vector", "memset", epsrow[:], Z_EPS, inc=(s_dve.h, 1))
    t_consts = s_dve.bump()
    op("sync", "dma_start", out=maskt[:], in_=mask_d, inc=(s_tr.h, 16))
    t_mask = s_tr.bump(16)

    for (dst, src) in ((bq_sb, bq_d), (bk_sb, bk_d), (bv_sb, bv_d), (bo_sb, bo_d)):
        op("gpsimd", "dma_start", out=dst[:], in_=src, inc=(s_gc.h, 16))
        s_gc.bump(16)
    t_bias = s_gc.n

    t_xc = np.zeros((SEQ, 4), dtype=np.int64)
    _xc_jobs = [(seq, ch) for seq in range(SEQ) for ch in range(4)]
    _xc_i = {"i": 0}

    def emit_xc(n):
        for _ in range(n):
            i = _xc_i["i"]
            if i >= len(_xc_jobs):
                return
            seq, ch = _xc_jobs[i]
            op("gpsimd", "dma_start", out=x_bf[seq, ch * 512:(ch + 1) * 512, :],
               in_=x_in[seq, ch * 512:(ch + 1) * 512, :], inc=(s_gc.h, 16))
            t_xc[seq, ch] = s_gc.bump(16)
            _xc_i["i"] += 1

    emit_xc(2)

    _wkv_jobs = [(dst, srcd, ic) for (dst, srcd) in ((wk_sb, wk_d), (wv_sb, wv_d))
                 for ic in range(16)]
    _wkv_i = {"i": 0}

    def emit_wkv(n):
        for _ in range(n):
            i = _wkv_i["i"]
            if i >= len(_wkv_jobs):
                return
            dst, srcd, ic = _wkv_jobs[i]
            op("gpsimd", "dma_start", out=dst[:, ic * 2048:(ic + 1) * 2048],
               in_=srcd[ic * 128:(ic + 1) * 128, :], inc=(s_gc.h, 16))
            s_gc.bump(16)
            _wkv_i["i"] += 1

    wgu_T = [0] * 32
    wd_T = [0] * 8
    _wc = {"g": 0, "d": 0}

    def emit_wcast(n):
        for _ in range(n):
            if _wc["g"] < 32:
                g = _wc["g"]
                op("gpsimd", "dma_start", out=wgu_bf[g * 512:(g + 1) * 512, :],
                   in_=wgu_d[g * 512:(g + 1) * 512, :], inc=(s_gc.h, 16))
                wgu_T[g] = s_gc.bump(16)
                _wc["g"] += 1
            elif _wc["d"] < 8:
                d = _wc["d"]
                op("gpsimd", "dma_start", out=wd_bf[d * 256:(d + 1) * 256, :],
                   in_=wd_d[d * 256:(d + 1) * 256, :], inc=(s_gc.h, 16))
                wd_T[d] = s_gc.bump(16)
                _wc["d"] += 1

    wq_free = [(s_mm.h, 0), (s_mm.h, 0)]
    qpan_T = np.zeros((SEQ, 2, NO), dtype=np.int64)
    kvpan_T = np.zeros((SEQ,), dtype=np.int64)
    t_grp_prev = 0
    qstg_free = (s_sc.h, 0)
    kstg_free = (s_sc.h, 0)
    vstg_free = (s_sc.h, 0)
    tmin_free = (s_act.h, 0)
    texp_free = (s_dve.h, 0)

    _wkv_done = [0]
    for seq in range(SEQ):
        for tg in range(2):
            xT_w = [(s_gc.h, int(t_xc[seq, 2 * tg + 1]))]
            if t_grp_prev:
                xT_w.append((s_mm.h, t_grp_prev))
            for i in range(16):
                op("sync", "dma_start", out=xT[:, i * 1024:(i + 1) * 1024],
                   in_=x_bf[seq, tg * 1024:(tg + 1) * 1024, i * 128:(i + 1) * 128],
                   transpose=True, w=(xT_w if i == 0 else ()), inc=(s_tr.h, 16))
                s_tr.bump(16)
            t_xT = s_tr.n
            t_xT2 = 0

            for o in range(NO):
                p = o % 2
                op("gpsimd", "dma_start", out=wq_c[p][:],
                   in_=wq_d[:, o * 128:(o + 1) * 128].rearrange("(ic p) o -> p ic o", p=128),
                   w=[wq_free[p]], inc=(s_gc.h, 16))
                t_wq = s_gc.bump(16)
                if seq == 0 and tg == 0:
                    if o < 6:
                        emit_xc(1)
                    emit_wkv(2)
                    if o == NO - 1:
                        _wkv_done[0] = s_gc.n

                banks = [2 * p, 2 * p + 1]
                mms = []
                for i in range(16):
                    for tt in range(2):
                        mms.append((psb[banks[tt]][:, :],
                                    wq_c[p][:, i * 128:(i + 1) * 128],
                                    xT[:, i * 1024 + tt * 512:i * 1024 + (tt + 1) * 512],
                                    i == 0, False))
                for tt in range(2):
                    mms.append((psb[banks[tt]][:, :],
                                bq_sb[0:1, o * 128:(o + 1) * 128],
                                ones512[0:1, 0:512], False, True))
                w = [(s_gc.h, t_wq), (s_tr.h, t_xT), (s_sc.h, t_xT2),
                     ps_free[banks[0]], ps_free[banks[1]]]
                if seq == 0 and tg == 0 and o == 0:
                    w += [(s_dve.h, t_consts), (s_gc.h, t_bias)]
                t_mms(mms, w=w, inc=(s_mm.h, 1))
                t_q = s_mm.bump()
                wq_free[p] = (s_mm.h, t_q)

                for tt in range(2):
                    b = banks[tt]
                    op("vector", "tensor_scalar_min", tmin[:], psb[b][:, :], 0.0,
                       w=[(s_mm.h, t_q), tmin_free], inc=(s_dve.h, 1))
                    t_min = s_dve.bump()
                    op("scalar", "activation", texp[:], tmin[:], AF.Exp,
                       w=[(s_dve.h, t_min), texp_free], inc=(s_act.h, 1))
                    t_exp = s_act.bump()
                    tmin_free = (s_act.h, t_exp)
                    op("vector", "scalar_tensor_tensor",
                       qstg[:, tt * 512:(tt + 1) * 512], psb[b][:, :], 0.0, texp[:],
                       op0=ALU.max, op1=ALU.add,
                       w=[(s_act.h, t_exp)] + ([qstg_free] if tt == 0 else []),
                       inc=(s_dve.h, 1))
                    t_stt = s_dve.bump()
                    texp_free = (s_dve.h, t_stt)
                    ps_free[b] = (s_dve.h, t_stt)
                op("scalar", "dma_start",
                   out=qf_s[seq, o, :, tg * 1024:(tg + 1) * 1024], in_=qstg[:],
                   w=[(s_dve.h, t_stt)], inc=(s_sc.h, 16))
                qpan_T[seq, tg, o] = s_sc.bump(16)
                qstg_free = (s_sc.h, int(qpan_T[seq, tg, o]))

            for cs in range(8):
                c0 = tg * 1024 + cs * 128
                xsl = lambda i: xT[:, i * 1024 + cs * 128:i * 1024 + cs * 128 + 128]
                mms = []
                for i in range(16):
                    for eg in range(4):
                        mms.append((psb[eg][:, :], xsl(i),
                                    wk_sb[:, i * 2048 + eg * 512:i * 2048 + (eg + 1) * 512],
                                    i == 0, False))
                    for eg in range(4):
                        mms.append((psb[4 + eg][:, :], xsl(i),
                                    wv_sb[:, i * 2048 + eg * 512:i * 2048 + (eg + 1) * 512],
                                    i == 0, False))
                for eg in range(4):
                    mms.append((psb[eg][:, :], ones512[0:1, 0:128],
                                bk_sb[0:1, eg * 512:(eg + 1) * 512], False, True))
                for eg in range(4):
                    mms.append((psb[4 + eg][:, :], ones512[0:1, 0:128],
                                bv_sb[0:1, eg * 512:(eg + 1) * 512], False, True))
                w = [ps_free[b] for b in range(8)] + [(s_gc.h, _wkv_done[0])]
                t_mms(mms, w=w, inc=(s_mm.h, 1))
                t_kv = s_mm.bump()

                for eg in range(4):
                    op("vector", "tensor_scalar_min", tmin[:], psb[eg][:, :], 0.0,
                       w=[(s_mm.h, t_kv), tmin_free], inc=(s_dve.h, 1))
                    t_min = s_dve.bump()
                    op("scalar", "activation", texp[:], tmin[:], AF.Exp,
                       w=[(s_dve.h, t_min), texp_free], inc=(s_act.h, 1))
                    t_exp = s_act.bump()
                    tmin_free = (s_act.h, t_exp)
                    op("vector", "scalar_tensor_tensor",
                       kstg[:, eg * 512:(eg + 1) * 512], psb[eg][:, :], 0.0, texp[:],
                       op0=ALU.max, op1=ALU.add,
                       w=[(s_act.h, t_exp)] + ([kstg_free] if eg == 0 else []),
                       inc=(s_dve.h, 1))
                    t_stt = s_dve.bump()
                    texp_free = (s_dve.h, t_stt)
                    ps_free[eg] = (s_dve.h, t_stt)
                    op("scalar", "activation", vstg[:, eg * 512:(eg + 1) * 512],
                       psb[4 + eg][:, :], AF.Copy,
                       w=[(s_mm.h, t_kv)] + ([vstg_free] if eg == 0 else []),
                       inc=(s_act.h, 1))
                    t_vc = s_act.bump()
                    ps_free[4 + eg] = (s_act.h, t_vc)
                op("scalar", "dma_start",
                   out=kf_s[seq, :, c0:c0 + 128, :].rearrange("h c e -> c h e"),
                   in_=kstg[:], w=[(s_dve.h, t_stt)], inc=(s_sc.h, 16))
                t_kw = s_sc.bump(16)
                kstg_free = (s_sc.h, t_kw)
                op("scalar", "dma_start",
                   out=v_s[seq, :, c0:c0 + 128, :].rearrange("h c e -> c h e"),
                   in_=vstg[:], w=[(s_act.h, t_vc)], inc=(s_sc.h, 16))
                t_vw = s_sc.bump(16)
                vstg_free = (s_sc.h, t_vw)
            t_grp_prev = s_mm.n
            emit_wcast(2)
        kvpan_T[seq] = s_sc.n

    if upto < 3:
        return finish()

    yst_T = np.zeros((SEQ,), dtype=np.int64)
    ystg_free = {0: (s_tr.h, 0), 1: (s_tr.h, 0)}
    p12_mm_end = s_mm.n
    tile_free = [(s_mm.h, p12_mm_end), (s_mm.h, p12_mm_end)]
    pair_mm_end = [p12_mm_end, p12_mm_end]
    trk = {}
    for seq in range(SEQ):
        pan_ready = int(max(kvpan_T[seq], qpan_T[seq].max()))

        if seq == 0:
            for par in range(2):
                op("vector", "memset",
                   v_h[par][:].rearrange("c (n e) -> c n e", e=129)[:, :, 128:129],
                   1.0, w=[(s_mm.h, p12_mm_end)], inc=(s_dve.h, 1))
            t_vones = s_dve.bump(2)

        for hp in range(8):
            mm_guard = s_mm.n
            dve_guard = (s_dve.h, s_dve.n)
            act_guard = (s_act.h, s_act.n)
            for hh in range(2):
                h = hp * 2 + hh
                par = hh
                op("scalar", "dma_start", out=qf_h[par][:], in_=qf_s[seq, h, :, :],
                   w=[tile_free[par], (s_sc.h, pan_ready)], inc=(s_sc.h, 16))
                s_sc.bump(16)
                op("scalar", "dma_start", out=kftm_h[par][:],
                   in_=kf_s[seq, h, :, :].rearrange("(n c) e -> c n e", c=128),
                   inc=(s_sc.h, 16))
                s_sc.bump(16)
                op("scalar", "dma_start",
                   out=v_h[par][:].rearrange("c (n e) -> c n e", e=129)[:, :, 0:128],
                   in_=v_s[seq, h, :, :].rearrange("(n c) e -> c n e", c=128),
                   inc=(s_sc.h, 16))
                ld_T = s_sc.bump(16)
                op("sync", "dma_start", out=kffm_h[par][:], in_=kf_s[seq, h, :, :],
                   transpose=True, w=[tile_free[par]], inc=(s_tr.h, 16))
                tr_T = s_tr.bump(16)

                op("vector", "memset", kvbf[par][:], 0.0,
                   w=[(s_mm.h, pair_mm_end[par])], inc=(s_dve.h, 1))
                t_state0 = s_dve.bump()

                trk[par] = dict(
                    kv_copy=(s_dve.h, t_state0),
                    mask=dve_guard,
                    amask_rd=(s_mm.h, mm_guard),
                    y_free={0: dve_guard, 1: dve_guard},
                    kvupd=(s_mm.h, mm_guard),
                    y_done=(s_mm.h, mm_guard),
                    ld=ld_T, tr=tr_T,
                )

            for ch in range(NCH):
                csl = slice(ch * 128, (ch + 1) * 128)
                vsl = slice(ch * 129, (ch + 1) * 129)
                tA, tY = {}, {}
                if ch > 0:
                    for hh in range(2):
                        tk = trk[hh]
                        op("scalar", "activation", kvbf[hh][:], psb[0 + hh][:, 0:129],
                           AF.Copy, w=[tk["kvupd"]], inc=(s_act.h, 1))
                        tk["kv_copy"] = (s_act.h, s_act.bump())
                for hh in range(2):
                    tk = trk[hh]
                    wA = [tk["mask"]]
                    if ch == 0:
                        wA += [(s_sc.h, tk["ld"]), (s_tr.h, tk["tr"]),
                               (s_tr.h, t_mask), act_guard, (s_dve.h, t_vones)]
                    t_mms([(psb[2 + hh][:, 0:128], kffm_h[hh][:, csl],
                            qf_h[hh][:, csl], True, True)], w=wA, inc=(s_mm.h, 1))
                    tA[hh] = s_mm.bump()
                for hh in range(2):
                    tk = trk[hh]
                    op("vector", "tensor_tensor", amask[hh][:], psb[2 + hh][:, 0:128],
                       maskt[:], op=ALU.mult,
                       w=[(s_mm.h, tA[hh]), tk["amask_rd"]], inc=(s_dve.h, 1))
                    tk["mask"] = (s_dve.h, s_dve.bump())
                for hh in range(2):
                    tk = trk[hh]
                    bY = 4 + 2 * hh + (ch % 2)
                    wy = [tk["mask"], tk["kv_copy"], tk["y_free"][ch % 2]]
                    if ch == 0:
                        wy.append(act_guard)
                    t_mms([
                        (psb[bY][:, 0:129], qf_h[hh][:, csl], kvbf[hh][:, 0:129],
                         True, False),
                        (psb[bY][:, 0:129], amask[hh][:], v_h[hh][:, vsl], False, False),
                        (psb[bY][:, 128:129], epsrow[0:1, :], ones512[0:1, 0:1],
                         False, True),
                    ], w=wy, inc=(s_mm.h, 1))
                    tY[hh] = s_mm.bump()
                    tk["amask_rd"] = (s_mm.h, tY[hh])
                    tk["y_done"] = (s_mm.h, tY[hh])
                for hh in range(2):
                    tk = trk[hh]
                    bY = 4 + 2 * hh + (ch % 2)
                    op("vector", "reciprocal", zinv[hh][:], psb[bY][:, 128:129],
                       w=[(s_mm.h, tY[hh])], inc=(s_dve.h, 1))
                    t_zi = s_dve.bump()
                    op("scalar", "activation",
                       ystg[ch % 2][:, hh * 128:(hh + 1) * 128], psb[bY][:, 0:128],
                       AF.Copy, scale=zinv[hh][:, 0:1],
                       w=[(s_dve.h, t_zi), ystg_free[ch % 2]],
                       inc=(s_act.h, 1))
                    tk["y_free"][ch % 2] = (s_act.h, s_act.bump())
                if ch < NCH - 1:
                    for hh in range(2):
                        tk = trk[hh]
                        wup = [tk["kv_copy"]]
                        if ch == 0:
                            wup += [act_guard, dve_guard]
                        t_mms([(psb[0 + hh][:, 0:129], kftm_h[hh][:, csl],
                                v_h[hh][:, vsl], ch == 0, ch == NCH - 2)],
                              w=wup, inc=(s_mm.h, 1))
                        tk["kvupd"] = (s_mm.h, s_mm.bump())

                op("sync", "dma_start",
                   out=y_s[seq, ch * 128:(ch + 1) * 128, hp * 256:(hp + 1) * 256],
                   in_=ystg[ch % 2][:],
                   w=[trk[0]["y_free"][ch % 2], trk[1]["y_free"][ch % 2]],
                   inc=(s_tr.h, 16))
                ystg_free[ch % 2] = (s_tr.h, s_tr.bump(16))

            for hh in range(2):
                tile_free[hh] = (s_mm.h, s_mm.n)
                pair_mm_end[hh] = s_mm.n
        yst_T[seq] = s_tr.n
        emit_wcast(10)

    p3_dve = (s_dve.h, s_dve.n)
    p3_act = (s_act.h, s_act.n)
    for b in range(8):
        ps_free[b] = p3_act if b < 4 else p3_dve
    ps_free[4] = ps_free[5] = p3_dve
    ps_free[6] = ps_free[7] = p3_dve
    ps_free[0] = ps_free[1] = p3_act
    ps_free[2] = ps_free[3] = p3_act

    if upto < 4:
        return finish()

    wo_free = [(s_mm.h, p12_mm_end), (s_mm.h, p12_mm_end)]
    attnst_T = np.zeros((SEQ,), dtype=np.int64)
    h1b_T = np.zeros((SEQ,), dtype=np.int64)
    h1f_T = np.zeros((SEQ, 16), dtype=np.int64)
    yT_loaded_mm = 0
    attnstg_free = (s_sc.h, 0)
    ln_guard = [(s_dve.h, 0)]
    ln_store_guard = []

    def emit_ln(in_bf_tile, in_f32_tile, w_in, out_f32, out_b16, stores):
        nonlocal ln_guard, ln_store_guard
        op("vector", "tensor_add", r1[:], in_bf_tile[:], in_f32_tile[:],
           w=list(w_in) + ln_guard, inc=(s_dve.h, 1))
        t_r = s_dve.bump()
        op("vector", "reduce_sum", lnst[:, 0:1], r1[:], axis=AX.X,
           w=[(s_dve.h, t_r)], inc=(s_dve.h, 1))
        t_s0 = s_dve.bump()
        op("scalar", "activation", sqt[:], r1[:], AF.Square,
           w=[(s_dve.h, t_r)], inc=(s_act.h, 1))
        t_sq = s_act.bump()
        op("vector", "reduce_sum", lnst[:, 1:2], sqt[:], axis=AX.X,
           w=[(s_act.h, t_sq), (s_dve.h, t_s0)], inc=(s_dve.h, 1))
        t_s1 = s_dve.bump()
        op("vector", "tensor_scalar_mul", lnst[:, 2:3], lnst[:, 0:1], 1.0 / DM,
           w=[(s_dve.h, t_s1)], inc=(s_dve.h, 1))
        t_m = s_dve.bump()
        op("vector", "tensor_scalar_mul", lnst[:, 3:4], lnst[:, 1:2], 1.0 / DM,
           w=[(s_dve.h, t_m)], inc=(s_dve.h, 1))
        t_m2 = s_dve.bump()
        op("vector", "tensor_tensor", lnst[:, 4:5], lnst[:, 2:3], lnst[:, 2:3],
           op=ALU.mult, w=[(s_dve.h, t_m2)], inc=(s_dve.h, 1))
        t_mm2 = s_dve.bump()
        op("vector", "tensor_sub", lnst[:, 5:6], lnst[:, 3:4], lnst[:, 4:5],
           w=[(s_dve.h, t_mm2)], inc=(s_dve.h, 1))
        t_v = s_dve.bump()
        op("vector", "tensor_scalar_add", lnst[:, 5:6], lnst[:, 5:6], LN_EPS,
           w=[(s_dve.h, t_v)], inc=(s_dve.h, 1))
        t_ve = s_dve.bump()
        op("scalar", "activation", lnst[:, 6:7], lnst[:, 5:6], AF.Sqrt,
           w=[(s_dve.h, t_ve)], inc=(s_act.h, 1))
        t_sd = s_act.bump()
        op("vector", "reciprocal", lnst[:, 7:8], lnst[:, 6:7],
           w=[(s_act.h, t_sd)], inc=(s_dve.h, 1))
        t_rs = s_dve.bump()
        op("vector", "tensor_scalar_mul", lnst[:, 8:9], lnst[:, 2:3], -1.0,
           w=[(s_dve.h, t_rs)], inc=(s_dve.h, 1))
        t_nm = s_dve.bump()
        op("vector", "tensor_scalar", out_f32[:], r1[:], lnst[:, 8:9], lnst[:, 7:8],
           op0=ALU.add, op1=ALU.mult,
           w=[(s_dve.h, t_nm)] + ln_store_guard, inc=(s_dve.h, 1))
        t_norm = s_dve.bump()
        t_bf = None
        if out_b16 is not None:
            op("vector", "tensor_copy", out_b16[:], out_f32[:],
               w=[(s_dve.h, t_norm)], inc=(s_dve.h, 1))
            t_bf = s_dve.bump()
        ln_guard = [(s_dve.h, s_dve.n)]
        ln_store_guard = stores(t_norm, t_bf)

    for seq in range(SEQ):
        for E in range(16):
            w = [(s_tr.h, int(yst_T[seq]))]
            if E == 0 and seq > 0:
                w.append((s_mm.h, yT_loaded_mm))
            op("sync", "dma_start", out=yT[:, E * 2048:(E + 1) * 2048],
               in_=y_s[seq, :, E * 128:(E + 1) * 128], transpose=True,
               w=w, inc=(s_tr.h, 16))
            s_tr.bump(16)
        t_yT = s_tr.n
        t_yT2 = 0

        for o in range(NO):
            p = o % 2
            op("gpsimd", "dma_start", out=wo_c[p][:],
               in_=wo_d[:, o * 128:(o + 1) * 128].rearrange("(ic p) o -> p ic o", p=128),
               w=[wo_free[p]], inc=(s_gc.h, 16))
            t_wo = s_gc.bump(16)

            banks = [4 * p + tt for tt in range(4)]
            mms = []
            for E in range(16):
                for tt in range(4):
                    mms.append((psb[banks[tt]][:, :],
                                wo_c[p][:, E * 128:(E + 1) * 128],
                                yT[:, E * 2048 + tt * 512:E * 2048 + (tt + 1) * 512],
                                E == 0, False))
            for tt in range(4):
                mms.append((psb[banks[tt]][:, :], bo_sb[0:1, o * 128:(o + 1) * 128],
                            ones512[0:1, 0:512], False, True))
            w = [(s_gc.h, t_wo), (s_tr.h, t_yT), (s_sc.h, t_yT2)] + [ps_free[b] for b in banks]
            t_mms(mms, w=w, inc=(s_mm.h, 1))
            t_woc = s_mm.bump()
            wo_free[p] = (s_mm.h, t_woc)

            for tt in range(4):
                b = banks[tt]
                op("scalar", "activation", attnstg[:, tt * 512:(tt + 1) * 512],
                   psb[b][:, :], AF.Copy,
                   w=[(s_mm.h, t_woc)] + ([attnstg_free] if tt == 0 else []),
                   inc=(s_act.h, 1))
                t_ev = s_act.bump()
                ps_free[b] = (s_act.h, t_ev)
            op("scalar", "dma_start",
               out=attn_s[o * 128:(o + 1) * 128, seq * 2048:(seq + 1) * 2048],
               in_=attnstg[:], w=[(s_act.h, t_ev)], inc=(s_sc.h, 16))
            attnstg_free = (s_sc.h, s_sc.bump(16))
        attnst_T[seq] = s_sc.n
        yT_loaded_mm = s_mm.n

        attn_tm_free = [(s_dve.h, 0)]
        for tc in range(16):
            tglob = seq * 2048 + tc * 128
            op("sync", "dma_start", out=attn_tm[:],
               in_=attn_s[:, tglob:tglob + 128], transpose=True,
               w=[(s_sc.h, int(attnst_T[seq]))] + attn_tm_free, inc=(s_tr.h, 16))
            t_at = s_tr.bump(16)
            op("scalar", "dma_start", out=x_tm[:],
               in_=x_in[seq, tc * 128:(tc + 1) * 128, :],
               w=attn_tm_free, inc=(s_sc.h, 16))
            t_xt = s_sc.bump(16)

            def stores(t_norm, t_bf, seq=seq, tc=tc, tglob=tglob):
                op("scalar", "dma_start", out=h1f_s[tglob:tglob + 128, :], in_=h1f[:],
                   w=[(s_dve.h, t_norm)], inc=(s_sc.h, 16))
                h1f_T[seq, tc] = s_sc.bump(16)
                op("scalar", "dma_start", out=h1b_s[tglob:tglob + 128, :], in_=h1b[:],
                   w=[(s_dve.h, t_bf)], inc=(s_sc.h, 16))
                return [(s_sc.h, s_sc.bump(16))]

            emit_ln(attn_tm, x_tm, [(s_tr.h, t_at), (s_sc.h, t_xt)], h1f, h1b, stores)
            attn_tm_free = [(s_dve.h, s_dve.n)]
        h1b_T[seq] = s_sc.n
        emit_wcast(6)
        emit_wcast(0)

    t_p4_mm_end = s_mm.n

    if upto < 5:
        return finish()

    emit_wcast(48)
    gate_T = np.zeros((64,), dtype=np.int64)
    a_T = np.zeros((64,), dtype=np.int64)
    wt_free = [(s_mm.h, t_p4_mm_end), (s_mm.h, t_p4_mm_end)]
    gstg_free = (s_sc.h, 0)
    ustg_free = (s_sc.h, 0)
    grd_free = (s_dve.h, 0)
    h1T_T = [0, 0]
    wdt_T = 0
    _wdt_jobs = [(og, jc) for og in range(4) for jc in range(64)]
    _wdt_i = {"i": 0}
    wdtr_free = [(s_gc.h, 0)] * 4
    wdt_T2 = 0

    def emit_wdt(n):
        nonlocal wdt_T, wdt_T2
        for _ in range(n):
            i = _wdt_i["i"]
            if i >= len(_wdt_jobs):
                return
            og, jc = _wdt_jobs[i]
            p = i % 4
            op("sync", "dma_start", out=wdtr[p][:],
               in_=wd_bf[og * 512:(og + 1) * 512, jc * 128:(jc + 1) * 128],
               transpose=True,
               w=[(s_gc.h, wd_T[og * 2 + 1]), wdtr_free[p],
                  (s_mm.h, t_p4_mm_end if i < 4 else 0)],
               inc=(s_tr.h, 16))
            t_t = s_tr.bump(16)
            op("gpsimd", "dma_start",
               out=wdt_s[og * 4:(og + 1) * 4, jc, :, :].rearrange("oc j o -> j oc o"),
               in_=wdtr[p][:], w=[(s_tr.h, t_t)], inc=(s_gc.h, 16))
            wdt_T2 = s_gc.bump(16)
            wdtr_free[p] = (s_gc.h, wdt_T2)
            wdt_T = wdt_T2
            _wdt_i["i"] += 1

    for th in range(2):
        for i in range(16):
            w = [(s_sc.h, int(h1b_T[th]))]
            if i == 0:
                w.append((s_mm.h, t_p4_mm_end if th == 0 else s_mm.n))
            op("sync", "dma_start", out=h1T[:, i * 2048:(i + 1) * 2048],
               in_=h1b_s[th * 2048:(th + 1) * 2048, i * 128:(i + 1) * 128],
               transpose=True, w=w, inc=(s_tr.h, 16))
            s_tr.bump(16)
        h1T_T[th] = s_tr.n
        h1T_T2 = 0

        for phase in range(2):
            for gg in range(16):
                g = phase * 16 + gg
                p = gg % 2
                for i in range(16):
                    w = [(s_gc.h, wgu_T[g])]
                    if i == 0:
                        w.append(wt_free[p])
                        if th == 0 and phase == 0 and gg == 0:
                            w.append((s_mm.h, t_p4_mm_end))
                    op("sync", "dma_start", out=wt5[p][:, i * 512:(i + 1) * 512],
                       in_=wgu_bf[g * 512:(g + 1) * 512, i * 128:(i + 1) * 128],
                       transpose=True, w=w, inc=(s_tr.h, 16))
                    s_tr.bump(16)
                t_wt = s_tr.n
                t_wt2 = 0

                for js in range(4):
                    jc = g * 4 + js
                    banks = [(js % 2) * 4 + tt for tt in range(4)]
                    mms = []
                    for i in range(16):
                        for tt in range(4):
                            mms.append((psb[banks[tt]][:, :],
                                        wt5[p][:, i * 512 + js * 128:i * 512 + (js + 1) * 128],
                                        h1T[:, i * 2048 + tt * 512:i * 2048 + (tt + 1) * 512],
                                        i == 0, i == 15))
                    w = [ps_free[b] for b in banks]
                    if js == 0:
                        w += [(s_tr.h, t_wt), (s_sc.h, t_wt2),
                              (s_tr.h, h1T_T[th]), (s_sc.h, h1T_T2)]
                    t_mms(mms, w=w, inc=(s_mm.h, 1))
                    t_g = s_mm.bump()
                    if js == 3:
                        wt_free[p] = (s_mm.h, t_g)

                    if phase == 0:
                        for tt in range(4):
                            b = banks[tt]
                            op("scalar", "activation", gstg[:, tt * 512:(tt + 1) * 512],
                               psb[b][:, :], AF.Silu,
                               w=[(s_mm.h, t_g)] + ([gstg_free] if tt == 0 else []),
                               inc=(s_act.h, 1))
                            t_ev = s_act.bump()
                            ps_free[b] = (s_act.h, t_ev)
                        op("scalar", "dma_start",
                           out=gate_s[jc, :, th * 2048:(th + 1) * 2048],
                           in_=gstg[:], w=[(s_act.h, t_ev)], inc=(s_sc.h, 16))
                        gate_T[jc] = s_sc.bump(16)
                        gstg_free = (s_sc.h, int(gate_T[jc]))
                    else:
                        jcu = jc - 64
                        op("scalar", "dma_start", out=grd[:],
                           in_=gate_s[jcu, :, th * 2048:(th + 1) * 2048],
                           w=[(s_sc.h, int(gate_T[jcu])), grd_free], inc=(s_sc.h, 16))
                        t_gr = s_sc.bump(16)
                        for tt in range(4):
                            b = banks[tt]
                            op("vector", "tensor_tensor",
                               ustg[:, tt * 512:(tt + 1) * 512], psb[b][:, :],
                               grd[:, tt * 512:(tt + 1) * 512], op=ALU.mult,
                               w=[(s_mm.h, t_g), (s_sc.h, t_gr)] +
                                 ([ustg_free] if tt == 0 else []),
                               inc=(s_dve.h, 1))
                            t_ev = s_dve.bump()
                            ps_free[b] = (s_dve.h, t_ev)
                        grd_free = (s_dve.h, t_ev)
                        op("scalar", "dma_start",
                           out=a_s[jcu, :, th * 2048:(th + 1) * 2048],
                           in_=ustg[:], w=[(s_dve.h, t_ev)], inc=(s_sc.h, 16))
                        a_T[jcu] = s_sc.bump(16)
                        ustg_free = (s_sc.h, int(a_T[jcu]))
                emit_wdt(4)

    t_p5_mm_end = s_mm.n
    emit_wdt(256)
    t_a_all = int(a_T.max())

    if upto < 6:
        return finish()

    wdt6_free = [(s_mm.h, 0), (s_mm.h, 0)]
    ffn_T = np.zeros((8,), dtype=np.int64)
    pair_mm6 = [0] * 4
    ffn_tm_free = [(s_dve.h, 0)]
    dstg_free = [(s_sc.h, 0), (s_sc.h, 0)]

    ap_T = {}

    def emit_apan(tp):
        for part in range(2):
            for half in range(2):
                tg = 2 * tp + half
                for q4 in range(4):
                    q8 = part * 4 + q4
                    w = []
                    if part == 0 and half == 0 and q4 == 0:
                        w = [(s_sc.h, t_a_all), (s_gc.h, wdt_T),
                             (s_mm.h, t_p5_mm_end if tp == 0 else pair_mm6[tp - 1])]
                    op("sync", "dma_start",
                       out=apan[half][:, q8 * 8 * 512:(q8 + 1) * 8 * 512],
                       in_=a_s[q8 * 8:(q8 + 1) * 8, :, tg * 512:(tg + 1) * 512]
                           .rearrange("jc j t -> j jc t"),
                       w=w, inc=(s_tr.h, 16))
                    s_tr.bump(16)
            if part == 0:
                t_lo = s_tr.n
        ap_T[tp] = (t_lo, s_tr.n)

    emit_apan(0)
    _ln2_jobs = []

    def emit_ln2(n):
        for _ in range(n):
            if not _ln2_jobs:
                return
            _ln2_jobs.pop(0)()

    for tp in range(4):
        t_ap_lo, t_ap_hi = ap_T[tp]

        for o in range(NO):
            bb = [(o % 2) * 2, (o % 2) * 2 + 1]
            t_d = None
            for hf in range(2):
                ph = (o * 2 + hf) % 2
                qn, sem = (("scalar", s_sc) if hf == 0 else ("sync", s_tr))
                op(qn, "dma_start", out=wdt6[ph][:],
                   in_=wdt_s[o, hf * 32:(hf + 1) * 32, :, :].rearrange("jc j o -> j jc o"),
                   w=[(s_gc.h, wdt_T), wdt6_free[ph]], inc=(sem.h, 16))
                t_wl = (sem.h, sem.bump(16))

                mms = []
                for jl in range(32):
                    jc = hf * 32 + jl
                    for half in range(2):
                        mms.append((psb[bb[half]][:, :],
                                    wdt6[ph][:, jl * 128:(jl + 1) * 128],
                                    apan[half][:, jc * 512:(jc + 1) * 512],
                                    jc == 0, jc == 63))
                w = [t_wl]
                if hf == 0:
                    w += [ps_free[bb[0]], ps_free[bb[1]]]
                    if o == 0:
                        w.append((s_tr.h, t_ap_lo))
                elif o == 0:
                    w.append((s_tr.h, t_ap_hi))
                t_mms(mms, w=w, inc=(s_mm.h, 1))
                t_d = s_mm.bump()
                wdt6_free[ph] = (s_mm.h, t_d)

            for half in range(2):
                tg = 2 * tp + half
                op("scalar", "activation", dstg[half][:], psb[bb[half]][:, :], AF.Copy,
                   w=[(s_mm.h, t_d), dstg_free[half]], inc=(s_act.h, 1))
                t_ev = s_act.bump()
                ps_free[bb[half]] = (s_act.h, t_ev)
                op("scalar", "dma_start",
                   out=ffn_s[o * 128:(o + 1) * 128, tg * 512:(tg + 1) * 512],
                   in_=dstg[half][:], w=[(s_act.h, t_ev)], inc=(s_sc.h, 16))
                dstg_free[half] = (s_sc.h, s_sc.bump(16))
            emit_ln2(1)
        ffn_T[2 * tp] = ffn_T[2 * tp + 1] = s_sc.n
        pair_mm6[tp] = s_mm.n
        ffn_T_now = int(ffn_T[2 * tp])

        def mk_ln2(tglob, ffn_T_now=ffn_T_now):
            def job():
                nonlocal ffn_tm_free
                seq, trow = tglob // 2048, tglob % 2048
                op("sync", "dma_start", out=ffn_tm[:],
                   in_=ffn_s[:, tglob:tglob + 128], transpose=True,
                   w=[(s_sc.h, ffn_T_now)] + ffn_tm_free, inc=(s_tr.h, 16))
                t_ft = s_tr.bump(16)
                op("gpsimd", "dma_start", out=h1f_rd[:],
                   in_=h1f_s[tglob:tglob + 128, :],
                   w=[(s_sc.h, int(h1f_T[seq, trow // 128]))] + ffn_tm_free,
                   inc=(s_gc.h, 16))
                t_hr = s_gc.bump(16)

                def stores2(t_norm, t_bf, seq=seq, trow=trow):
                    op("sync", "dma_start", out=out_d[seq, trow:trow + 128, :],
                       in_=outstg[:], w=[(s_dve.h, t_norm)], inc=(s_out.h, 16))
                    return [(s_out.h, s_out.bump(16))]

                emit_ln(ffn_tm, h1f_rd, [(s_tr.h, t_ft), (s_gc.h, t_hr)],
                        outstg, None, stores2)
                ffn_tm_free = [(s_dve.h, s_dve.n)]
            return job

        for q4 in range(8):
            _ln2_jobs.append(mk_ln2(tp * 1024 + q4 * 128))
        if tp < 3:
            emit_apan(tp + 1)
        else:
            emit_ln2(64)

    emit_ln2(64)
    op("sync", "wait_ge", s_out.h, s_out.n)
    return finish()


def kernel(L0_attn_x, L0_attn_Wq, L0_attn_bq, L0_attn_Wk, L0_attn_bk,
           L0_attn_Wv, L0_attn_bv, L0_attn_Wo, L0_attn_bo,
           L0_ffn_w_gate_up, L0_ffn_w_down, _upto=6, _dbg=(), _cores=8):
    x = np.ascontiguousarray(np.asarray(L0_attn_x, dtype=np.float32))
    wq = np.ascontiguousarray(np.asarray(L0_attn_Wq, dtype=np.float32))
    wk = np.ascontiguousarray(np.asarray(L0_attn_Wk, dtype=np.float32))
    wv = np.ascontiguousarray(np.asarray(L0_attn_Wv, dtype=np.float32))
    wo = np.ascontiguousarray(np.asarray(L0_attn_Wo, dtype=np.float32))
    bq = np.ascontiguousarray(np.asarray(L0_attn_bq, dtype=np.float32).reshape(1, DM))
    bk = np.ascontiguousarray(np.asarray(L0_attn_bk, dtype=np.float32).reshape(1, DM))
    bv = np.ascontiguousarray(np.asarray(L0_attn_bv, dtype=np.float32).reshape(1, DM))
    bo = np.ascontiguousarray(np.asarray(L0_attn_bo, dtype=np.float32).reshape(1, DM))
    wgu = np.ascontiguousarray(np.asarray(L0_ffn_w_gate_up, dtype=np.float32))
    wd = np.ascontiguousarray(np.asarray(L0_ffn_w_down, dtype=np.float32))
    cmask = np.triu(np.ones((128, 128))).astype(ml_dtypes.bfloat16)

    nc, dbg_outs = build(upto=_upto, dbg=_dbg)

    in_maps = []
    for c in range(_cores):
        in_maps.append(dict(
            x=x[2 * c:2 * c + 2], wq=wq, wk=wk, wv=wv, wo=wo,
            bq=bq, bk=bk, bv=bv, bo=bo, wgu=wgu, wd=wd, cmask=cmask,
        ))
    res = run_bass_kernel_spmd(nc, in_maps, core_ids=list(range(_cores)))
    if _dbg:
        return res
    return np.concatenate([r["out"] for r in res.results], axis=0)
